# version 01
# baseline (speedup 1.0000x reference)
import numpy as np
import ml_dtypes
import concourse.bass as bass
import concourse.mybir as mybir
from concourse.bass_utils import run_bass_kernel_spmd
from contextlib import ExitStack

F32 = mybir.dt.float32
BF16 = mybir.dt.bfloat16
U8 = mybir.dt.uint8
AF = mybir.ActivationFunctionType
ALU = mybir.AluOpType
AX = mybir.AxisListType
BFNP = ml_dtypes.bfloat16

T = 2048
D = 1024
NCH = 8
TCW = 512
NTC = T // TCW
H = 16
DH = 64
E = 8
FH = 512
FHC = FH // 128
STATE = 16
PI = float(np.pi)

STAGE = 4
DEBUG = False
GROUPS_OVERRIDE = [(0, 1, 2), (3, 4, 5), (6, 7, 8)]
PARANOID_FFN = True
LN_PARANOID_STATS = False
LN_PARANOID_ROWS = False
LN_PARANOID_APPLY = True
LN_APPLY_WAITG = True


class Plan:
    ENGS = ("t", "v", "s", "g", "y")
    NAME = {"t": "tensor", "v": "vector", "s": "scalar", "g": "gpsimd", "y": "sync"}

    def __init__(self, nc, ctx):
        self.nc = nc
        self.ops = {e: [] for e in self.ENGS}
        self.sem = {e: ctx.enter_context(nc.semaphore(f"S{e}")) for e in self.ENGS}
        self.cnt = {e: 0 for e in self.ENGS}
        self.lastw = {e: {} for e in self.ENGS}

    paranoid = False

    def put(self, e, fn, waits=(), sig=False, inc=1):
        if self.paranoid:
            waits = list(waits) + [(pe, self.cnt[pe]) for pe in self.ENGS
                                   if self.cnt[pe] > 0]
            sig = True
        w2 = []
        for tok in waits:
            if tok is None:
                continue
            pe, v = tok
            if pe == e:
                continue
            if self.lastw[e].get(pe, 0) >= v:
                continue
            self.lastw[e][pe] = v
            w2.append((pe, v))
        tok = None
        if sig:
            self.cnt[e] += inc
            tok = (e, self.cnt[e])
        self.ops[e].append((fn, w2, inc if sig else 0))
        return tok

    def emit(self, block):
        semmap = self.sem
        for e in self.ENGS:
            ops = self.ops[e]
            sem_self = self.sem[e]

            def make(ops=ops, sem_self=sem_self):
                def run(eng):
                    for fn, waits, inc in ops:
                        for pe, v in waits:
                            eng.wait_ge(semmap[pe], v)
                        ins = fn(eng)
                        if inc:
                            ins.then_inc(sem_self, inc)
                return run
            getattr(block, self.NAME[e])(make())


def _dma(out, in_, **kw):
    return lambda e: e.dma_start(out=out, in_=in_, **kw)


def _mm(out, l, r, st, sp):
    return lambda e: e.matmul(out, l, r, start=st, stop=sp)


def _tr(out, in_, ident):
    return lambda e: e.transpose(out, in_, ident)


def _act(out, in_, func, bias=0.0, scale=1.0):
    return lambda e: e.activation(out, in_, func, bias=bias, scale=scale)


def _v1(name, *args, **kw):
    return lambda e: getattr(e, name)(*args, **kw)


def build(stage=None):
    if stage is None:
        stage = STAGE
    nc = bass.Bass("TRN2", target_bir_lowering=False)
    ctx = ExitStack()

    def din(name, shape, dt=F32):
        return nc.dram_tensor(name, shape, dt, kind="ExternalInput").ap()

    x_in = din("L0_attn_a_x", [2, T, D])
    A_log = din("L0_attn_a_A_log", [STATE])
    B_in = din("L0_attn_a_B", [STATE, D])
    C_in = din("L0_attn_a_C", [D, STATE])
    D_in = din("L0_attn_a_D", [D])
    Wd_in = din("L0_attn_a_W_delta", [D, D])
    bd_in = din("L0_attn_a_b_delta", [D])
    wq_in = din("L0_attn_b_wq", [D, D])
    wk_in = din("L0_attn_b_wk", [D, D])
    wv_in = din("L0_attn_b_wv", [D, D])
    wo_in = din("L0_attn_b_wo", [D, D])
    wph_in = din("L0_attn_b_w_phase", [D, 2])
    car_in = din("L0_attn_b_carrier", [H])
    lock_in = din("L0_attn_b_lock", [H])
    bw_in = din("L0_attn_b_bandwidth", [H])
    am_in = din("L0_attn_b_active_mask", [H], U8)
    rw_in = din("L0_ffn_routing_W", [E, D])
    eb_in = din("L0_ffn_expert_bias", [E])
    sW1_in = din("L0_ffn_sW1", [D, FH])
    sb1_in = din("L0_ffn_sb1", [FH])
    sW2_in = din("L0_ffn_sW2", [FH, D])
    sb2_in = din("L0_ffn_sb2", [D])
    rW1_in = din("L0_ffn_rW1", [E, D, FH])
    rb1_in = din("L0_ffn_rb1", [E, FH])
    rW2_in = din("L0_ffn_rW2", [E, FH, D])
    rb2_in = din("L0_ffn_rb2", [E, D])
    idf_in = din("c_idf", [128, 128])
    idb_in = din("c_idb", [128, 128], BF16)
    masku_in = din("c_masku", [128, 128], BF16)
    sel8_in = din("c_sel8", [8, 1024], BF16)
    out_d = nc.dram_tensor("out", [2, T, D], F32, kind="ExternalOutput").ap()
    dbg_xn = nc.dram_tensor("dbg_xn", [128, 512], F32, kind="ExternalOutput").ap()
    dbg_dl = nc.dram_tensor("dbg_dl", [1, 2048], F32, kind="ExternalOutput").ap()
    dbg_hs = nc.dram_tensor("dbg_hs", [16, 2048], F32, kind="ExternalOutput").ap()
    dbg_da = nc.dram_tensor("dbg_da", [16, 2048], F32, kind="ExternalOutput").ap()
    dbg_db = nc.dram_tensor("dbg_db", [16, 2048], F32, kind="ExternalOutput").ap()
    dbg_bt = nc.dram_tensor("dbg_bt", [128, 128], F32, kind="ExternalOutput").ap()
    dbg_bx = nc.dram_tensor("dbg_bx", [16, 512], F32, kind="ExternalOutput").ap()
    dbg_cmb = nc.dram_tensor("dbg_cmb", [8, 512], F32, kind="ExternalOutput").ap()
    dbg_ht = nc.dram_tensor("dbg_ht", [128, 512], F32, kind="ExternalOutput").ap()
    dbg_ya = nc.dram_tensor("dbg_ya", [128, 512], F32, kind="ExternalOutput").ap()
    dbg_yb = nc.dram_tensor("dbg_yb", [128, 512], F32, kind="ExternalOutput").ap()
    dbg_acr = nc.dram_tensor("dbg_acr", [8, 512], F32, kind="ExternalOutput").ap()
    dbg_atm = nc.dram_tensor("dbg_atm", [128, 32], F32, kind="ExternalOutput").ap()
    dbg_tk = nc.dram_tensor("dbg_tk", [128, 8], F32, kind="ExternalOutput").ap()
    dbg_rw = nc.dram_tensor("dbg_rw", [128, 4096], F32, kind="ExternalOutput").ap()
    dbg_cs = nc.dram_tensor("dbg_cs", [16, 4], F32, kind="ExternalOutput").ap()
    dbg_g = nc.dram_tensor("dbg_g", [16, 512], F32, kind="ExternalOutput").ap()

    P = Plan(nc, ctx)

    def sb(name, shape, dt):
        return ctx.enter_context(nc.sbuf_tensor(name, shape, dt))

    def psum(name, shape, dt=F32):
        return ctx.enter_context(nc.psum_tensor(name, shape, dt))

    XR = sb("XR", [128, NCH * T], F32)
    AR1 = sb("AR1", [128, 16384], BF16)
    AR2 = sb("AR2", [128, 16384], BF16)
    AR3 = sb("AR3", [128, 16640], BF16)
    AR4 = sb("AR4", [128, 8192], BF16)
    AR5 = sb("AR5", [128, 4096], BF16)
    XN = sb("XN", [128, NCH * TCW], BF16)
    IDF = sb("IDF", [128, 128], F32)
    IDB = sb("IDB", [128, 128], BF16)
    MSKU = sb("MSKU", [128, 128], BF16)
    ONE1B = sb("ONE1B", [1, 128], BF16)
    ONE16F = sb("ONE16F", [1, 16], F32)
    ONE64B = sb("ONE64B", [1, 64], BF16)
    ONECB = sb("ONECB", [128, 1], BF16)
    BCOL = sb("BCOL", [128, NCH], F32)
    DCOL = sb("DCOL", [128, NCH], F32)
    SB1C = sb("SB1C", [128, FHC], F32)
    SB2C = sb("SB2C", [128, NCH], F32)
    RB1C = sb("RB1C", [128, E * FHC], F32)
    EBC = sb("EBC", [8, 1], F32)
    ALC = sb("ALC", [16, 1], F32)
    ACOL = sb("ACOL", [16, 1], F32)
    AMU = sb("AMU", [16, 1], U8)
    AMF = sb("AMF", [16, 1], F32)
    CARC = sb("CARC", [16, 1], F32)
    LOCKC = sb("LOCKC", [16, 1], F32)
    BWC = sb("BWC", [16, 1], F32)
    COSC = sb("COSC", [16, 1], F32)
    SINC = sb("SINC", [16, 1], F32)
    IBWC = sb("IBWC", [16, 1], F32)
    L1C = sb("L1C", [16, 1], F32)
    CCOL = sb("CCOL", [128, 3], F32)
    CC16 = sb("CC16", [16, 8], F32)
    AMR = sb("AMR", [1, 16], F32)
    AMH = sb("AMH", [1, 1], F32)
    INVSR = sb("INVSR", [1, TCW], BF16)
    SGBF = sb("SGBF", [64, TCW], BF16)
    INVS2 = SGBF[0:1, :]
    T16A = sb("T16A", [16, 1], F32)
    T16B = sb("T16B", [16, 1], F32)
    WPH = sb("WPH", [128, NCH * 2], BF16)
    MROW = sb("MROW", [1, TCW], BF16)
    QROW = sb("QROW", [1, TCW], BF16)
    RROW = sb("RROW", [1, TCW], BF16)
    MRROW = sb("MRROW", [1, TCW], BF16)
    TROW = sb("TROW", [1, TCW], BF16)
    XBT = [sb("XBT0", [128, TCW], BF16)] * 2
    XQT = [sb("XQT0", [128, TCW], BF16)] * 2
    TMP1 = sb("TMP1", [128, TCW], F32)
    TKM = [sb(f"TKM{i}", [128, 1], F32) for i in range(2)]
    TK8 = [sb(f"TK8{i}", [128, 8], F32) for i in range(3)]
    ATM = sb("ATM", [128, 32], F32)

    PA = psum("PA", [128, 512])
    PB = psum("PB", [128, 512])
    PC = psum("PC", [128, 512])
    PD = psum("PD", [128, 512])
    PO = psum("PO", [128, 512])
    PE_ = psum("PEx", [128, 512])
    PF = psum("PFx", [128, 512])
    PT = psum("PTx", [128, 1024], BF16)

    AR1F = AR1[:].bitcast(F32)
    AR2F = AR2[:].bitcast(F32)
    AR3F = AR3[:].bitcast(F32)
    AR4F = AR4[:].bitcast(F32)
    AR5F = AR5[:].bitcast(F32)

    bank_tok = {}

    def bfree(b):
        return bank_tok.get(id(b))

    def bset(b, tok):
        bank_tok[id(b)] = tok

    g, y, t, v, s = "g", "y", "t", "v", "s"

    def DMA(q, out, in_, waits=(), **kw):
        return P.put(q, _dma(out, in_, **kw), waits, sig=True, inc=16)

    with nc.allow_non_contiguous_dma(reason="tiny column loads"):
        tk_idf = DMA(y, IDF[:], idf_in)
        tk_idb = DMA(y, IDB[:], idb_in)
        DMA(y, MSKU[:], masku_in)
        tk_cols = DMA(y, BCOL[:], bd_in.rearrange("(c p) -> p c", p=128), allow_slow_non_contiguous=True)
        DMA(y, DCOL[:], D_in.rearrange("(c p) -> p c", p=128), allow_slow_non_contiguous=True)
        DMA(y, SB1C[:], sb1_in.rearrange("(c p) -> p c", p=128), allow_slow_non_contiguous=True)
        DMA(y, SB2C[:], sb2_in.rearrange("(c p) -> p c", p=128), allow_slow_non_contiguous=True)
        DMA(y, RB1C[:].rearrange("p (e c) -> p e c", e=E),
            rb1_in.rearrange("e (c p) -> p e c", p=128), allow_slow_non_contiguous=True)
        DMA(y, EBC[:], eb_in[:, None])
        DMA(y, ALC[:], A_log[:, None])
        DMA(y, AMU[:], am_in[:, None])
        DMA(y, CARC[:], car_in[:, None])
        DMA(y, LOCKC[:], lock_in[:, None])
        tk_small = DMA(y, BWC[:], bw_in[:, None])
    tk_wph = DMA(g, WPH[:].rearrange("p (c f) -> p c f", c=NCH),
                 wph_in.rearrange("(c p) f -> p c f", p=128))

    tk = P.put(s, _act(T16A[:], ALC[:], AF.Exp), [tk_small], sig=True)
    tk_acol = P.put(s, _act(ACOL[:], T16A[:], AF.Identity, scale=-1.0), [tk], sig=True)
    tk_amf = P.put(v, _v1("tensor_copy", AMF[:], AMU[:]), [tk_small], sig=True)
    tk_amr = P.put(t, _tr(PE_[0:1, 0:16], AMF[:], IDF[0:16, 0:16]),
                   [tk_amf, tk_idf], sig=True)
    tk_amr = P.put(v, _v1("tensor_copy", AMR[:], PE_[0:1, 0:16]), [tk_amr], sig=True)
    bset(PE_, tk_amr)
    P.put(v, _v1("memset", CC16[:, 0:1], -0.5))
    P.put(v, _v1("memset", CC16[:, 1:2], -0.25))
    P.put(v, _v1("memset", CC16[:, 2:3], -1.5))
    P.put(v, _v1("memset", CC16[:, 3:4], -1.25))
    P.put(v, _v1("memset", CC16[:, 4:5], -2 * PI))
    P.put(v, _v1("memset", CC16[:, 5:6], -1.5 * PI))
    P.put(v, _v1("memset", CC16[:, 6:7], 1.0 + 1e-6))
    tk_cc16 = P.put(v, _v1("memset", CC16[:, 7:8], 0.0), sig=True)

    def sbounce(tok):
        return P.put(v, _v1("engine_nop"), [tok], sig=True)

    def trig_path(dst, boff, breg):
        tk = P.put(s, _act(T16A[:], CARC[:], AF.Identity, scale=1.0 / (2 * PI),
                           bias=CC16[:, boff:boff + 1]), [tk_small, tk_cc16],
                   sig=True)
        tk = P.put(s, _act(T16B[:], T16A[:], AF.Sign), [tk], sig=True)
        tk = P.put(s, _act(T16A[:], CARC[:], AF.Identity, scale=1.0 / (2 * PI),
                           bias=CC16[:, boff + 2:boff + 3]), [tk], sig=True)
        tk = P.put(s, _act(dst, T16A[:], AF.Sign), [tk], sig=True)
        tk = P.put(s, _act(T16A[:], T16B[:], AF.Identity, scale=-PI,
                           bias=CC16[:, breg:breg + 1]), [tk], sig=True)
        tk = P.put(s, _act(T16B[:], dst, AF.Identity, scale=-PI,
                           bias=T16A[:]), [sbounce(tk)], sig=True)
        tk = P.put(s, _act(T16A[:], CARC[:], AF.Identity, bias=T16B[:]),
                   [sbounce(tk)], sig=True)
        tk = P.put(s, _act(dst, T16A[:], AF.Sin), [tk], sig=True)
        return tk

    tk_sinc = trig_path(SINC[:], 0, 4)
    tk_cosc = trig_path(COSC[:], 1, 5)
    tk = P.put(s, _act(T16A[:], BWC[:], AF.Ln), [tk_small, tk_cosc], sig=True)
    tk_ibw = P.put(s, _act(IBWC[:], T16A[:], AF.Exp, scale=-1.0), [tk], sig=True)
    tk_l1c = P.put(s, _act(L1C[:], LOCKC[:], AF.Identity, scale=-1.0,
                           bias=CC16[:, 6:7]), [tk_small, tk_cc16], sig=True)
    P.put(v, _v1("memset", CCOL[:, 0:1], 1e-5))
    P.put(v, _v1("memset", CCOL[:, 2:3], 1e-30))
    tk_ccol = P.put(v, _v1("memset", CCOL[:, 1:2], -12.0), sig=True)
    tk = P.put(v, _v1("memset", ONE1B[:], 1.0), sig=True)
    P.put(v, _v1("memset", ONE16F[:], 1.0))
    P.put(v, _v1("memset", ONE64B[:], 1.0))
    tk_ones = P.put(v, _v1("memset", ONECB[:], 1.0), sig=True)

    state = {"xr": None}

    def LN(b, tc, extra_waits=()):
        outer_paranoid = True
        P.paranoid = outer_paranoid and LN_PARANOID_STATS
        xr_tok = state["xr"]
        sum_tok = q_tok = None
        cast_cons = [None, None]
        for dc in range(NCH):
            pp = dc % 2
            xs = XR[:, dc * T + tc * TCW: dc * T + (tc + 1) * TCW]
            ck = P.put(v, _v1("tensor_copy", XBT[pp][:], xs),
                       [xr_tok, cast_cons[pp]] + list(extra_waits), sig=True)
            sq = P.put(s, _act(XQT[pp][:], xs, AF.Square),
                       [xr_tok, cast_cons[pp]] + list(extra_waits), sig=True)
            last = dc == NCH - 1
            sum_tok = P.put(t, _mm(PE_[0:1, 0:TCW], ONECB[:], XBT[pp][:],
                                   dc == 0, last),
                            [ck, tk_ones, bfree(PE_)], sig=last)
            q_tok = P.put(t, _mm(PF[0:1, 0:TCW], ONECB[:], XQT[pp][:],
                                 dc == 0, last),
                          [sq, bfree(PF)], sig=True)
            cast_cons[pp] = q_tok
        P.paranoid = outer_paranoid and LN_PARANOID_ROWS
        tkm = P.put(v, _v1("tensor_scalar_mul", MROW[:], PE_[0:1, 0:TCW], 1.0 / D),
                    [sum_tok], sig=True)
        bset(PE_, tkm)
        tkq = P.put(v, _v1("tensor_scalar_mul", QROW[:], PF[0:1, 0:TCW], 1.0 / D),
                    [q_tok], sig=True)
        bset(PF, tkq)
        tk1 = P.put(v, _v1("tensor_mul", TROW[:], MROW[:], MROW[:]), [tkm], sig=True)
        tk2 = P.put(v, _v1("tensor_sub", QROW[:], QROW[:], TROW[:]), [tk1], sig=True)
        tk3 = P.put(s, _act(TROW[:], QROW[:], AF.Sqrt, bias=CCOL[0:1, 0:1]), [tk2], sig=True)
        tk4 = P.put(v, _v1("reciprocal", RROW[:], TROW[:]), [tk3], sig=True)
        tk5 = P.put(v, _v1("tensor_mul", MRROW[:], MROW[:], RROW[:]), [tk4], sig=True)
        mmr = P.put(t, _mm(PC[:, 0:TCW], ONE1B[:], RROW[:], True, True),
                    [tk4, bfree(PC)], sig=True)
        mmm = P.put(t, _mm(PD[:, 0:TCW], ONE1B[:], MRROW[:], True, True),
                    [tk5, bfree(PD)], sig=True)
        P.paranoid = outer_paranoid and LN_PARANOID_APPLY
        xn_tok = None
        for dc in range(NCH):
            xs = XR[:, dc * T + tc * TCW: dc * T + (tc + 1) * TCW]
            wg = [("g", P.cnt["g"])] if (outer_paranoid and LN_APPLY_WAITG
                                         and P.cnt["g"] > 0) else []
            a1 = P.put(v, _v1("tensor_mul", TMP1[:], xs, PC[:, 0:TCW]),
                       [mmr] + wg, sig=False)
            xn_tok = P.put(v, _v1("tensor_sub", XN[:, dc * TCW:(dc + 1) * TCW],
                                  TMP1[:], PD[:, 0:TCW]),
                           [mmm], sig=(dc == NCH - 1))
        bset(PC, xn_tok)
        bset(PD, xn_tok)
        P.paranoid = outer_paranoid
        return xn_tok

    def io_in(b):
        STG = [AR5F[:, 0:1024], AR5F[:, 1024:2048]]
        dma_tok = [None, None]
        cons_tok = state.get("stg_cons", [None, None])
        xr_view = XR[:].rearrange("p (c t) -> p c t", c=NCH)
        last = None
        for tt in range(T // 128):
            pp = tt % 2
            dma_tok[pp] = DMA(y, STG[pp], x_in[b, tt * 128:(tt + 1) * 128, :],
                              waits=[cons_tok[pp]])
            for half in range(2):
                bank = PE_ if half == 0 else PF
                trt = None
                for j in range(4):
                    dc = half * 4 + j
                    trt = P.put(t, _tr(bank[:, j * 128:(j + 1) * 128],
                                       STG[pp][:, dc * 128:(dc + 1) * 128], IDF[:]),
                                [dma_tok[pp], tk_idf, bfree(bank)],
                                sig=(j == 3))
                cp = P.put(v, _v1(
                    "tensor_copy",
                    xr_view[:, half * 4:(half + 1) * 4, tt * 128:(tt + 1) * 128],
                    bank[:, 0:512].rearrange("p (c q) -> p c q", c=4)),
                    [trt], sig=True)
                bset(bank, cp)
                cons_tok[pp] = cp
                last = cp
        state["xr"] = last

    def io_out(b):
        STG = [AR5F[:, 0:1024], AR5F[:, 1024:2048]]
        xr_view = XR[:].rearrange("p (c t) -> p c t", c=NCH)
        dma_cons = [None, None]
        final_toks = []
        for tt in range(T // 128):
            pp = tt % 2
            cp = None
            for half in range(2):
                bank = PE_ if half == 0 else PF
                trt = None
                for j in range(4):
                    dc = half * 4 + j
                    trt = P.put(t, _tr(bank[:, j * 128:(j + 1) * 128],
                                       XR[:, dc * T + tt * 128: dc * T + (tt + 1) * 128],
                                       IDF[:]),
                                [state["xr"], bfree(bank), dma_cons[pp]],
                                sig=(j == 3))
                cp = P.put(v, _v1(
                    "tensor_copy",
                    STG[pp][:, half * 512:(half + 1) * 512]
                    .rearrange("p (c q) -> p c q", c=4),
                    bank[:, 0:512].rearrange("p (c q) -> p c q", c=4)),
                    [trt], sig=True)
                bset(bank, cp)
            dtok = DMA(y, out_d[b, tt * 128:(tt + 1) * 128, :], STG[pp], waits=[cp])
            dma_cons[pp] = dtok
            final_toks.append(dtok)
        state["stg_cons"] = list(dma_cons)
        return final_toks

    def ssm(b):
        dA = AR1F[0:16, 0:2048]
        HS = AR1F[0:16, 2048:4096]
        dBxB = AR1[0:16, 8192:10240]
        HSB = AR1[0:16, 10240:12288]
        DELT = AR1[0:1, 12288:14336]
        EZ = [AR1F[:, 7168:7680], AR1F[:, 7680:8192]]
        SPT = [AR4[:, 0:512], AR4[:, 512:1024]]
        WdN = [AR2[:, 0:1024], AR2[:, 1024:2048]]
        WdT = AR2[:, 2048:10240]
        BTv = AR2[:, 10240:10368]
        CTv = AR2[0:16, 10368:11392]
        BN = AR2[0:16, 11392:12416]
        CN = AR2[:, 12416:12432]
        DBS = AR2F[0:16, 6272:6784]

        tkB = DMA(g, BN, B_in, waits=[state["xr"]])
        trt = None
        for c in range(NCH):
            trt = P.put(t, _tr(PT[:, c * 16:(c + 1) * 16],
                               BN[:, c * 128:(c + 1) * 128], IDB[0:16, 0:16]),
                        [tkB, tk_idb, bfree(PT)], sig=(c == NCH - 1))
        tkBT = P.put(v, _v1("tensor_copy", BTv, PT[:, 0:128]), [trt], sig=True)
        bset(PT, tkBT)
        if DEBUG and b == 0:
            cbt = P.put(v, _v1("tensor_copy", TMP1[:, 0:128], BTv), [tkBT], sig=True)
            DMA(y, dbg_bt, TMP1[:, 0:128], waits=[cbt])
        ct_toks = []
        prev_cn = tkBT
        for c in range(NCH):
            tkC = DMA(g, CN, C_in[c * 128:(c + 1) * 128, :], waits=[prev_cn])
            trt = P.put(t, _tr(PT[0:16, c * 128:(c + 1) * 128], CN, IDB[:]),
                        [tkC, bfree(PT)], sig=True)
            prev_cn = trt
            ct_toks.append(trt)
        tkCT = P.put(v, _v1("tensor_copy", CTv, PT[0:16, 0:1024]), ct_toks, sig=True)
        bset(PT, tkCT)
        wd_evac = None
        for ec in range(NCH):
            pp = ec % 2
            tkW = DMA(g, WdN[pp][:], Wd_in[ec * 128:(ec + 1) * 128, :],
                      waits=[wd_evac, state["xr"]])
            trt = None
            for dc in range(NCH):
                trt = P.put(t, _tr(PT[:, dc * 128:(dc + 1) * 128],
                                   WdN[pp][:, dc * 128:(dc + 1) * 128], IDB[:]),
                            [tkW, bfree(PT)], sig=(dc == NCH - 1))
            wd_evac = P.put(v, _v1(
                "tensor_copy",
                WdT.rearrange("p (d e) -> p d e", d=NCH)[:, :, ec * 128:(ec + 1) * 128],
                PT[:, 0:1024].rearrange("p (d q) -> p d q", d=NCH)),
                [trt], sig=True)
            bset(PT, wd_evac)

        for tc in range(NTC):
            xn_tok = LN(b, tc, extra_waits=[wd_evac] if tc == 0 else [])
            if DEBUG and b == 0 and tc == 0:
                cpdbg = P.put(v, _v1("tensor_copy", TMP1[:], XN[:, 0:TCW]),
                              [xn_tok], sig=True)
                DMA(y, dbg_xn, TMP1[:], waits=[cpdbg])
            ds_tok = None
            ez_cons = [None, None]
            for ec in range(NCH):
                bankz = PA if ec % 2 == 0 else PB
                mmt = None
                for dc in range(NCH):
                    l = WdT[:, (dc * NCH + ec) * 128:(dc * NCH + ec + 1) * 128]
                    mmt = P.put(t, _mm(bankz[:, 0:TCW], l,
                                       XN[:, dc * TCW:(dc + 1) * TCW],
                                       dc == 0, dc == NCH - 1),
                                [xn_tok, wd_evac, bfree(bankz)],
                                sig=(dc == NCH - 1))
                pp = ec % 2
                e1 = P.put(s, _act(EZ[pp], bankz[:, 0:TCW], AF.Exp,
                                   bias=BCOL[:, ec:ec + 1]),
                           [mmt, tk_cols, ez_cons[pp]], sig=True)
                bset(bankz, e1)
                e2 = P.put(s, _act(SPT[pp][:], EZ[pp], AF.Ln, bias=1.0),
                           [e1], sig=True)
                ds_tok = P.put(t, _mm(PE_[0:1, 0:TCW], ONECB[:], SPT[pp][:],
                                      ec == 0, ec == NCH - 1),
                               [e2, bfree(PE_)], sig=(ec == NCH - 1))
                ez_cons[pp] = ds_tok
            dl_tok = P.put(v, _v1("tensor_scalar_mul",
                                  DELT[:, tc * TCW:(tc + 1) * TCW],
                                  PE_[0:1, 0:TCW], 1.0 / D),
                           [ds_tok], sig=True)
            bset(PE_, dl_tok)
            for dc in range(NCH):
                xs = XR[:, dc * T + tc * TCW: dc * T + (tc + 1) * TCW]
                tk_x = P.put(v, _v1("scalar_tensor_tensor", xs,
                                    XN[:, dc * TCW:(dc + 1) * TCW],
                                    DCOL[:, dc:dc + 1], xs, ALU.mult, ALU.add),
                             [xn_tok], sig=(dc == NCH - 1))
            state["xr"] = tk_x
            bx_tok = None
            for dc in range(NCH):
                bx_tok = P.put(t, _mm(PC[0:16, 0:TCW],
                                      BTv.rearrange("p (c n) -> p c n", c=NCH)[:, dc, :],
                                      XN[:, dc * TCW:(dc + 1) * TCW],
                                      dc == 0, dc == NCH - 1),
                               [xn_tok, tkBT, bfree(PC)], sig=(dc == NCH - 1))
            db_tok = P.put(t, _mm(PD[0:16, 0:TCW], ONE1B[0:1, 0:16],
                                  DELT[:, tc * TCW:(tc + 1) * TCW], True, True),
                           [dl_tok, bfree(PD)], sig=True)
            if DEBUG and b == 0 and tc == 0:
                cbx = P.put(v, _v1("tensor_copy", AR3F[0:16, 3072:3584],
                                   PC[0:16, 0:TCW]), [bx_tok], sig=True)
                DMA(y, dbg_bx, AR3F[0:16, 3072:3584], waits=[cbx])
            cp_tok = P.put(v, _v1("tensor_copy", DBS, PD[0:16, 0:TCW]),
                           [db_tok], sig=True)
            bset(PD, cp_tok)
            t1 = P.put(v, _v1("tensor_scalar_mul", EZ[0][0:16, 0:TCW], DBS, ACOL[:]),
                       [cp_tok, tk_acol], sig=True)
            tda = P.put(s, _act(dA[:, tc * TCW:(tc + 1) * TCW], EZ[0][0:16, 0:TCW],
                                AF.Exp), [t1], sig=True)
            tdb = P.put(v, _v1("tensor_mul", dBxB[:, tc * TCW:(tc + 1) * TCW],
                               PC[0:16, 0:TCW], DBS), [bx_tok, cp_tok], sig=True)
            bset(PC, tdb)
            if tc == NTC - 1:
                scan_in = [tda, tdb]
        tk_scan = P.put(v, _v1("tensor_tensor_scan", HS, dA, dBxB, 0.0,
                               ALU.mult, ALU.add), scan_in, sig=True)
        tk_hsb = P.put(v, _v1("tensor_copy", HSB, HS), [tk_scan], sig=True)
        if DEBUG and b == 0:
            DMA(y, dbg_hs, HS, waits=[tk_scan])
            DMA(y, dbg_da, dA, waits=[tk_scan])
            cdb1 = P.put(v, _v1("tensor_copy", AR3F[0:16, 4096:6144][0:1, 0:2048],
                                DELT), [tk_scan], sig=True)
            DMA(y, dbg_dl, AR3F[0:16, 4096:6144][0:1, 0:2048], waits=[cdb1])
            cdb2 = P.put(v, _v1("tensor_copy", AR3F[0:16, 0:2048], dBxB),
                         [tk_scan], sig=True)
            DMA(y, dbg_db, AR3F[0:16, 0:2048], waits=[cdb2])
        xr_last = None
        for tc in range(NTC):
            for dc in range(NCH):
                bank = PA if (tc * NCH + dc) % 2 == 0 else PB
                mmt = P.put(t, _mm(bank[:, 0:TCW],
                                   CTv.rearrange("p (c q) -> p c q", c=NCH)[:, dc, :],
                                   HSB[:, tc * TCW:(tc + 1) * TCW], True, True),
                            [tk_hsb, tkCT, bfree(bank)], sig=True)
                xs = XR[:, dc * T + tc * TCW: dc * T + (tc + 1) * TCW]
                xr_last = P.put(v, _v1("tensor_add", xs, xs, bank[:, 0:TCW]),
                                [mmt, state["xr"]], sig=True)
                bset(bank, xr_last)
        state["xr"] = xr_last

    GROUPS = GROUPS_OVERRIDE

    def ffn_unit_w(u):
        if u == 0:
            return sW1_in, sW2_in
        return rW1_in[u - 1], rW2_in[u - 1]

    def ffn(b, call):
        RW1 = AR1[:, 0:12288]
        RW2 = AR2[:, 0:12288]
        XNH = [AR1[:, 12288 + dc * 1024: 12288 + (dc + 1) * 1024] if dc < 4
               else AR2[:, 12288 + (dc - 4) * 1024: 12288 + (dc - 3) * 1024]
               for dc in range(NCH)]
        HT = AR3[:, 0:6144]
        YACC = [AR4F[:, 0:4096], AR3F[:, 3072:7168]]
        RTv = AR5[:, 0:64]
        RB2v = AR5[0:8, 64:1088]
        RWN8 = AR5[0:8, 1088:2112]
        CMB = AR5[0:8, 2112:3136]
        ACR = [AR3F[0:8, 7168:7680], AR3F[0:8, 7680:8192]]

        ffn_free = state["xr"]
        P.paranoid = PARANOID_FFN
        tk_rwn = DMA(g, RWN8, rw_in, waits=[ffn_free])
        tk_rb2 = DMA(g, RB2v, rb2_in, waits=[ffn_free])
        trt = None
        for c in range(NCH):
            trt = P.put(t, _tr(PT[:, c * 8:(c + 1) * 8],
                               RWN8[:, c * 128:(c + 1) * 128], IDB[0:8, 0:8]),
                        [tk_rwn, bfree(PT)], sig=(c == NCH - 1))
        tk_rt = P.put(v, _v1("tensor_copy", RTv, PT[:, 0:64]), [trt], sig=True)
        bset(PT, tk_rt)
        SEL8 = AR5[0:8, 1088:2112]
        tk_sel = DMA(y, SEL8, sel8_in, waits=[tk_rt])

        ng = len(GROUPS)
        w1tok = {}
        w2tok = {}

        def load_w1(gi, waits):
            for si, u in enumerate(GROUPS[gi]):
                w1, _ = ffn_unit_w(u)
                tok = DMA(g, RW1[:, si * 4096:(si + 1) * 4096]
                          .rearrange("p (c f) -> p c f", c=NCH),
                          w1.rearrange("(c p) f -> p c f", p=128), waits=waits)
            w1tok[gi] = tok

        def load_w2(gi, waits):
            for si, u in enumerate(GROUPS[gi]):
                _, w2 = ffn_unit_w(u)
                tok = DMA(g, RW2[:, si * 4096:(si + 1) * 4096]
                          .rearrange("p (c f) -> p c f", c=FHC),
                          w2.rearrange("(c p) f -> p c f", p=128), waits=waits)
            w2tok[gi] = tok

        load_w1(0, [ffn_free])
        load_w2(0, [ffn_free])

        xr_last = state["xr"]
        for half in range(2):
            P.paranoid = PARANOID_FFN
            xn_toks = {}
            cmb_toks = {}
            for tci in range(2):
                tc = half * 2 + tci
                P.paranoid = PARANOID_FFN
                xn_tok = LN(b, tc)
                P.paranoid = False
                cps = None
                for dc in range(NCH):
                    cps = P.put(v, _v1(
                        "tensor_copy", XNH[dc][:, tci * 512:(tci + 1) * 512],
                        XN[:, dc * TCW:(dc + 1) * TCW]),
                        [xn_tok, state.get("h1_last_mm")], sig=(dc == NCH - 1))
                xn_toks[tci] = cps
                lg = None
                for dc in range(NCH):
                    lg = P.put(t, _mm(PF[0:8, 0:TCW],
                                      RTv.rearrange("p (c e) -> p c e", c=NCH)[:, dc, :],
                                      XN[:, dc * TCW:(dc + 1) * TCW],
                                      dc == 0, dc == NCH - 1),
                               [xn_tok, tk_rt, bfree(PF)], sig=(dc == NCH - 1))
                a1 = P.put(s, _act(ACR[0], PF[0:8, 0:TCW], AF.Exp, bias=EBC[:]),
                           [lg], sig=True)
                bset(PF, a1)
                a2 = P.put(s, _act(ACR[1], ACR[0], AF.Ln, bias=1.0), [a1], sig=True)
                a3 = P.put(s, _act(ACR[0], ACR[1], AF.Sqrt), [a2], sig=True)
                if DEBUG and b == 0 and call == 0 and tc == 0:
                    DMA(y, dbg_acr, ACR[0], waits=[a3])
                trt = None
                for j in range(4):
                    trt = P.put(t, _tr(PE_[:, j * 8:(j + 1) * 8],
                                       ACR[0][:, j * 128:(j + 1) * 128], IDF[0:8, 0:8]),
                                [a3, bfree(PE_)], sig=(j == 3))
                cpa = P.put(v, _v1("tensor_copy", ATM[:], PE_[:, 0:32]), [trt],
                            sig=True)
                bset(PE_, cpa)
                if DEBUG and b == 0 and call == 0 and tc == 0:
                    DMA(y, dbg_atm, ATM[:], waits=[cpa])
                cb_tok = None
                prev_tmk = state.get("tmk")
                prev_trb = state.get("trb")
                for j in range(4):
                    at = ATM[:, j * 8:(j + 1) * 8]
                    tmx = P.put(v, _v1("max", TK8[0][:], at), [cpa, prev_tmk],
                                sig=True)
                    tm2 = P.put(s, _act(TKM[0][:], TK8[0][:, 1:2], AF.Identity),
                                [tmx], sig=True)
                    td = P.put(v, _v1("tensor_scalar_sub", TK8[1][:], at,
                                      TKM[0][:]), [tm2, prev_trb], sig=True)
                    tr_ = P.put(s, _act(TK8[2][:], TK8[1][:], AF.Relu,
                                        scale=1e12, bias=1.0), [td], sig=True)
                    P.put(s, _act(TK8[0][:], TK8[2][:], AF.Relu,
                                  scale=-1.0, bias=1.0), [])
                    tmk = P.put(s, _act(TK8[2][:], TK8[0][:], AF.Identity,
                                        scale=-1.0, bias=1.0), [], sig=True)
                    prev_tmk = tmk
                    state["tmk"] = tmk
                    tk1 = P.put(v, _v1("tensor_mul", TK8[1][:], at, TK8[2][:]),
                                [tmk], sig=True)
                    trb = P.put(t, _tr(PF[0:8, j * 128:(j + 1) * 128],
                                       TK8[1][:], IDF[:]),
                                [tk1, bfree(PF)], sig=True)
                    prev_trb = trb
                    state["trb"] = trb
                    cb_tok = P.put(v, _v1(
                        "tensor_copy",
                        CMB[:, tci * 512 + j * 128: tci * 512 + (j + 1) * 128],
                        PF[0:8, j * 128:(j + 1) * 128]), [trb], sig=True)
                    bset(PF, cb_tok)
                    if DEBUG and b == 0 and call == 0 and tc == 0 and j == 0:
                        DMA(y, dbg_tk, TK8[1][:], waits=[tk1])
                cmb_toks[tci] = cb_tok
                if DEBUG and b == 0 and call == 0 and tc == 0:
                    DMA(g, dbg_cmb, CMB[0:8, 0:512], waits=[cb_tok])

            for gi in range(ng):
                gl = GROUPS[gi]
                h1tok = {}
                last_h1_scale = None
                for tci in range(2):
                    tc = half * 2 + tci
                    for si, u in enumerate(gl):
                        if u > 0:
                            cbt = P.put(t, _mm(PE_[:, 0:TCW],
                                               SEL8[0:8, (u - 1) * 128:u * 128],
                                               CMB[0:8, tci * 512:(tci + 1) * 512],
                                               True, True),
                                        [cmb_toks[tci], tk_sel, bfree(PE_)], sig=True)
                        for hc in range(FHC):
                            bank = PA if (si * FHC + hc) % 2 == 0 else PB
                            mmt = None
                            for dc in range(NCH):
                                l = RW1[:, si * 4096 + dc * 512 + hc * 128:
                                        si * 4096 + dc * 512 + (hc + 1) * 128]
                                mmt = P.put(t, _mm(bank[:, 0:TCW], l,
                                                   XNH[dc][:, tci * 512:(tci + 1) * 512],
                                                   dc == 0, dc == NCH - 1),
                                            [xn_toks[tci], w1tok[gi], bfree(bank)],
                                            sig=(dc == NCH - 1))
                            state["h1_last_mm"] = mmt
                            hslot = HT[:, si * 2048 + hc * 512:
                                       si * 2048 + (hc + 1) * 512]
                            bias = SB1C[:, hc:hc + 1] if u == 0 else \
                                RB1C[:, (u - 1) * FHC + hc:(u - 1) * FHC + hc + 1]
                            sil = P.put(s, _act(hslot, bank[:, 0:TCW], AF.Silu,
                                                bias=bias), [mmt, tk_cols], sig=True)
                            bset(bank, sil)
                            if u > 0:
                                sil = P.put(v, _v1("tensor_mul", hslot, hslot,
                                                   PE_[:, 0:TCW]), [sil, cbt],
                                            sig=True)
                            h1tok[(si, hc)] = sil
                            last_h1_scale = sil
                            if (DEBUG and b == 0 and call == 0 and half == 0
                                    and tci == 0 and gi == 0 and si == 1
                                    and hc == 0):
                                DMA(g, dbg_ht, hslot, waits=[sil])
                                DMA(g, dbg_rw, RW1[:, 4096:8192], waits=[sil])
                        if u > 0:
                            bset(PE_, last_h1_scale)
                    for fc in range(NCH):
                        bank = PC if fc % 2 == 0 else PD
                        mms = []
                        for si2 in range(len(gl)):
                            for hc in range(FHC):
                                l = RW2[:, si2 * 4096 + hc * 1024 + fc * 128:
                                        si2 * 4096 + hc * 1024 + (fc + 1) * 128]
                                hslot = HT[:, si2 * 2048 + hc * 512:
                                           si2 * 2048 + (hc + 1) * 512]
                                mms.append((l, hslot, [h1tok[(si2, hc)],
                                                       w2tok[gi]]))
                        if gi == 0:
                            mms.append((RB2v[:, fc * 128:(fc + 1) * 128],
                                        CMB[:, tci * 512:(tci + 1) * 512],
                                        [tk_rb2, cmb_toks[tci]]))
                        mmt = None
                        for i, (l, r, ew) in enumerate(mms):
                            mmt = P.put(t, _mm(bank[:, 0:TCW], l, r,
                                               i == 0, i == len(mms) - 1),
                                        list(ew) + [bfree(bank)],
                                        sig=(i == len(mms) - 1))
                        ya = YACC[fc // 4][:, (fc % 4) * 1024 + tci * 512:
                                           (fc % 4) * 1024 + (tci + 1) * 512]
                        xs = XR[:, fc * T + tc * TCW: fc * T + (tc + 1) * TCW]
                        if gi == 0:
                            ev = P.put(s, _act(ya, bank[:, 0:TCW], AF.Identity,
                                               bias=SB2C[:, fc:fc + 1]),
                                       [mmt, state.get(("yacons", fc, tci))],
                                       sig=True)
                        elif gi == 1:
                            ev = P.put(v, _v1("tensor_add", ya, ya, bank[:, 0:TCW]),
                                       [mmt, state.get(("ya", fc, tci))], sig=True)
                        else:
                            e1 = P.put(v, _v1("tensor_add", TMP1[:], ya,
                                              bank[:, 0:TCW]),
                                       [mmt, state.get(("ya", fc, tci))], sig=False)
                            ev = P.put(v, _v1("tensor_add", xs, xs, TMP1[:]),
                                       [xr_last], sig=True)
                            xr_last = ev
                            state[("yacons", fc, tci)] = ev
                        if (DEBUG and b == 0 and call == 0 and half == 0
                                and tci == 0 and fc == 0 and gi <= 1):
                            DMA(y, dbg_ya if gi == 0 else dbg_yb, ya,
                                waits=[ev])
                        state[("ya", fc, tci)] = ev
                        bset(bank, ev)
                if not (half == 1 and gi == ng - 1):
                    nxt = (gi + 1) % ng
                    load_w1(nxt, [state["h1_last_mm"]])
                    load_w2(nxt, [state[("ya", NCH - 1, 1)]])
        P.paranoid = False
        state["xr"] = xr_last
        state["ffn_free"] = xr_last


    def attn(b):
        QO = AR1
        KF = AR2
        VT = AR3[:, 0:16384]
        QPAD = AR4[:, 0:2048]
        PTB = [AR4[:, 2048:2560], AR4[:, 2560:3072]]
        GTC = AR4[0:16, 3072:3584]
        GT12 = AR4F[:, 1792:2048]
        PH0R = AR4F[0:1, 2048:2560]
        PH1R = AR4F[0:1, 2560:3072]
        CDv = AR4F[0:16, 3072:3584]
        CDBv = AR4F[0:16, 3584:4096]
        RAv = CDv[0:1, :]
        RBv = CDBv[0:1, :]
        WSL = [AR5[:, 0:2048], AR5[:, 2048:4096]]
        arena_free = state["xr"]
        slot_cons = [arena_free, arena_free]
        slot_i = 0
        proj_toks = []

        def wslice(win, q4):
            return win[:, q4 * 256:(q4 + 1) * 256].rearrange(
                "(c p) f -> p c f", p=128)

        for tc in range(NTC):
            xn_tok = LN(b, tc, extra_waits=[arena_free] if tc == 0 else [])
            for wi, (w_in, dst) in enumerate([(wq_in, QO), (wk_in, KF)]):
                for q4 in range(4):
                    sl = slot_i % 2
                    slot_i += 1
                    wtok = DMA(g, WSL[sl][:].rearrange("p (c f) -> p c f", c=NCH),
                               wslice(w_in, q4), waits=[slot_cons[sl]])
                    for fc2 in range(2):
                        fcg = q4 * 2 + fc2
                        bank = PA if fcg % 2 == 0 else PB
                        mmt = None
                        for dc in range(NCH):
                            l = WSL[sl][:, dc * 256 + fc2 * 128:
                                        dc * 256 + (fc2 + 1) * 128]
                            mmt = P.put(t, _mm(bank[:, 0:TCW], l,
                                               XN[:, dc * TCW:(dc + 1) * TCW],
                                               dc == 0, dc == NCH - 1),
                                        [xn_tok, wtok, bfree(bank)],
                                        sig=(dc == NCH - 1))
                        slot_cons[sl] = mmt
                        ev = P.put(v, _v1(
                            "tensor_copy",
                            dst[:, fcg * T + tc * TCW: fcg * T + (tc + 1) * TCW],
                            bank[:, 0:TCW]), [mmt], sig=True)
                        bset(bank, ev)
                        proj_toks.append(ev)
            for q4 in range(4):
                sl = slot_i % 2
                slot_i += 1
                wtok = DMA(g, WSL[sl][:].rearrange("p (c f) -> p c f", c=NCH),
                           wslice(wv_in, q4), waits=[slot_cons[sl]])
                for tt in range(4):
                    bank = PA if tt % 2 == 0 else PB
                    mmt = None
                    for dc in range(NCH):
                        l = XN[:, dc * TCW + tt * 128: dc * TCW + (tt + 1) * 128]
                        mmt = P.put(t, _mm(bank[:, 0:256], l,
                                           WSL[sl][:, dc * 256:(dc + 1) * 256],
                                           dc == 0, dc == NCH - 1),
                                    [xn_tok, wtok, bfree(bank)],
                                    sig=(dc == NCH - 1))
                    slot_cons[sl] = mmt
                    kt = tc * 4 + tt
                    ev = P.put(v, _v1(
                        "tensor_copy",
                        VT[:, kt * 1024: (kt + 1) * 1024]
                        .rearrange("p (h c) -> p h c", c=64)[:, q4 * 4:(q4 + 1) * 4, :],
                        bank[:, 0:256].rearrange("p (h c) -> p h c", c=64)),
                        [mmt], sig=True)
                    bset(bank, ev)
                    proj_toks.append(ev)
            ph0 = ph1 = None
            for dc in range(NCH):
                ph0 = P.put(t, _mm(PE_[0:1, 0:TCW], WPH[:, dc * 2:dc * 2 + 1],
                                   XN[:, dc * TCW:(dc + 1) * TCW],
                                   dc == 0, dc == NCH - 1),
                            [xn_tok, tk_wph, bfree(PE_)], sig=(dc == NCH - 1))
                ph1 = P.put(t, _mm(PF[0:1, 0:TCW], WPH[:, dc * 2 + 1:dc * 2 + 2],
                                   XN[:, dc * TCW:(dc + 1) * TCW],
                                   dc == 0, dc == NCH - 1),
                            [bfree(PF)], sig=(dc == NCH - 1))
            t0 = P.put(v, _v1("tensor_copy", PH0R, PE_[0:1, 0:TCW]), [ph0], sig=True)
            bset(PE_, t0)
            t1 = P.put(v, _v1("tensor_copy", PH1R, PF[0:1, 0:TCW]), [ph1], sig=True)
            bset(PF, t1)
            t2 = P.put(v, _v1("tensor_mul", RAv, PH0R, PH0R), [t0])
            t2 = P.put(v, _v1("tensor_mul", RBv, PH1R, PH1R), [t1])
            t2 = P.put(v, _v1("tensor_add", RAv, RAv, RBv), [], sig=True)
            t3 = P.put(s, _act(RBv, RAv, AF.Sqrt, bias=CCOL[0:1, 2:3]), [t2], sig=True)
            t4 = P.put(v, _v1("reciprocal", RAv, RBv), [t3], sig=True)
            tu = P.put(v, _v1("tensor_mul", PH1R, PH1R, RAv), [t4], sig=True)
            tv = P.put(v, _v1("tensor_mul", PH0R, PH0R, RAv), [t4], sig=True)
            ub = P.put(t, _mm(PE_[0:16, 0:TCW], ONE16F[:], PH1R, True, True),
                       [tu, bfree(PE_)], sig=True)
            vb = P.put(t, _mm(PF[0:16, 0:TCW], ONE16F[:], PH0R, True, True),
                       [tv, bfree(PF)], sig=True)
            c1 = P.put(v, _v1("tensor_scalar_mul", CDv, PE_[0:16, 0:TCW],
                              COSC[:]), [ub], sig=True)
            bset(PE_, c1)
            c2 = P.put(v, _v1("scalar_tensor_tensor", CDv, PF[0:16, 0:TCW],
                              SINC[:], CDv, ALU.mult, ALU.add), [c1], sig=True)
            bset(PF, c2)
            c3 = P.put(v, _v1("tensor_scalar_add", CDBv, CDv, -1.0), [c2])
            c4 = P.put(v, _v1("tensor_scalar_mul", CDv, CDBv, IBWC[:]), [],
                       sig=True)
            c5 = P.put(s, _act(CDBv, CDv, AF.Exp), [c4], sig=True)
            c6 = P.put(v, _v1("tensor_scalar_mul", CDv, CDBv, LOCKC[:]), [c5])
            c7 = P.put(v, _v1("tensor_scalar_add", CDBv, CDv, L1C[:]), [],
                       sig=True)
            c8 = P.put(s, _act(GTC, CDBv, AF.Ln), [c7], sig=True)
            if DEBUG and b == 0 and tc == 0:
                DMA(g, dbg_g, GTC, waits=[c8])
            trt = None
            for j in range(4):
                trt = P.put(t, _tr(PT[:, j * 16:(j + 1) * 16],
                                   GTC[:, j * 128:(j + 1) * 128],
                                   IDB[0:16, 0:16]),
                            [c8, bfree(PT)], sig=(j == 3))
            gevs = None
            for j in range(4):
                kt = tc * 4 + j
                gevs = P.put(s, _act(GT12[:, kt * 16:(kt + 1) * 16],
                                     PT[:, j * 16:(j + 1) * 16], AF.Identity,
                                     bias=CCOL[:, 1:2]), [trt],
                             sig=(j == 3))
            bset(PT, gevs)
            proj_toks.append(gevs)

        pd = proj_toks
        last_smm = None
        last_sg = None
        amh_cons = None
        pt_cons = [None, None]
        for pc in range(NCH):
            for par in range(2):
                h = 2 * pc + par
                amh_tok = P.put(s, _act(AMH[:], AMR[0:1, h:h + 1], AF.Identity),
                                [tk_amr, amh_cons], sig=True)
                mz = P.put(v, _v1("memset",
                                  QPAD[(1 - par) * 64:(2 - par) * 64, :], 0.0),
                           [last_smm] + pd, sig=True)
                qp = P.put(v, _v1("tensor_copy",
                                  QPAD[par * 64:(par + 1) * 64, :],
                                  QO[par * 64:(par + 1) * 64,
                                     pc * T:(pc + 1) * T]),
                           [mz], sig=True)
                for qc in range(NTC):
                    obank = PO if qc % 2 == 0 else PC
                    sbank = PF if qc % 2 == 0 else PD
                    kmax = 4 * qc + 3
                    for ki in range(kmax + 1):
                        pp = ki % 2
                        sb_ = PA if pp == 0 else PB
                        diag = ki - 4 * qc
                        colsA = max(diag, 0) * 128
                        smm = P.put(t, _mm(sb_[:, colsA:512],
                                           KF[:, pc * T + ki * 128:
                                              pc * T + (ki + 1) * 128],
                                           QPAD[:, qc * TCW + colsA:
                                                (qc + 1) * TCW],
                                           True, True),
                                    [qp, bfree(sb_)], sig=True)
                        last_smm = smm
                        ex = P.put(s, _act(PTB[pp][:, colsA:512],
                                           sb_[:, colsA:512], AF.Exp, scale=0.125,
                                           bias=GT12[:, ki * 16 + h:
                                                     ki * 16 + h + 1]),
                                   [smm, pt_cons[pp]], sig=True)
                        bset(sb_, ex)
                        dtok = ex
                        if diag >= 0:
                            dtok = P.put(v, _v1("tensor_mul",
                                               PTB[pp][:, colsA:colsA + 128],
                                               PTB[pp][:, colsA:colsA + 128],
                                               MSKU[:]), [ex], sig=True)
                            if colsA > 0:
                                dtok = P.put(v, _v1("memset",
                                                    PTB[pp][:, 0:colsA], 0.0),
                                             [ex], sig=True)
                        av = P.put(t, _mm(obank[0:64, 0:512],
                                          VT[:, (ki * 16 + h) * 64:
                                             (ki * 16 + h + 1) * 64],
                                          PTB[pp][:, 0:512],
                                          ki == 0, ki == kmax),
                                   [dtok, bfree(obank)], sig=False)
                        sg_ = P.put(t, _mm(sbank[0:1, 0:512], ONECB[:],
                                           PTB[pp][:, 0:512],
                                           ki == 0, ki == kmax),
                                    [bfree(sbank)], sig=True)
                        pt_cons[pp] = sg_
                        last_sg = sg_
                    rc = P.put(v, _v1("reciprocal", INVSR[:],
                                      sbank[0:1, 0:512]), [last_sg], sig=True)
                    bset(sbank, rc)
                    ms = P.put(v, _v1("tensor_scalar_mul", INVS2, INVSR[:],
                                      AMH[:]), [rc, amh_tok], sig=True)
                    amh_cons = ms
                    bc = P.put(t, _mm(PE_[0:64, 0:512], ONE64B[:],
                                      INVS2, True, True),
                               [ms, bfree(PE_)], sig=True)
                    cp = P.put(v, _v1("tensor_copy", SGBF[:], PE_[0:64, 0:512]),
                               [bc], sig=True)
                    bset(PE_, cp)
                    dst = (QO if par == 0 else KF)[0:64,
                                                  pc * T + qc * TCW:
                                                  pc * T + (qc + 1) * TCW]
                    om = P.put(v, _v1("tensor_mul", dst, obank[0:64, 0:512],
                                      SGBF[:]), [cp], sig=True)
                    bset(obank, om)
                    state["last_om"] = om

        xr_last = state["xr"]
        wo_cons = [last_sg, last_sg]
        for q4 in range(4):
            sl = q4 % 2
            woslot = AR3[:, sl * 4096:(sl + 1) * 4096]
            wts = []
            for h in range(16):
                wts.append(DMA(g, woslot[0:64, h * 256:(h + 1) * 256],
                               wo_in[h * 64:(h + 1) * 64,
                                     q4 * 256:(q4 + 1) * 256],
                               waits=[wo_cons[sl], last_sg]))
            wtok = wts[-1]
            for fc2 in range(2):
                fcg = q4 * 2 + fc2
                for tc in range(NTC):
                    bank = PC if (fcg * NTC + tc) % 2 == 0 else PD
                    mmt = None
                    for h in range(16):
                        l = woslot[0:64, h * 256 + fc2 * 128:
                                   h * 256 + (fc2 + 1) * 128]
                        r = (QO if h % 2 == 0 else KF)[0:64,
                                                       (h // 2) * T + tc * TCW:
                                                       (h // 2) * T + (tc + 1) * TCW]
                        mmt = P.put(t, _mm(bank[:, 0:TCW], l, r,
                                           h == 0, h == 15),
                                    [wtok, bfree(bank), state["last_om"]],
                                    sig=(h == 15))
                    wo_cons[sl] = mmt
                    xs = XR[:, fcg * T + tc * TCW: fcg * T + (tc + 1) * TCW]
                    xr_last = P.put(v, _v1("tensor_add", xs, xs, bank[:, 0:TCW]),
                                    [mmt, xr_last], sig=True)
                    bset(bank, xr_last)
        state["xr"] = xr_last
        state["ffn_free"] = xr_last

    final_toks = []
    with ctx:
        with nc.allow_low_precision(reason="bf16 matmuls with fp32 accumulation"):
            with nc.Block() as block:
                for b in range(2):
                    io_in(b)
                    if stage >= 1:
                        ssm(b)
                    if stage >= 2:
                        ffn(b, 0)
                    if stage >= 3:
                        attn(b)
                    if stage >= 4:
                        ffn(b, 1)
                    final_toks += io_out(b)
                P.put(y, lambda e: e.nop(), final_toks)
                P.emit(block)
    return nc


def kernel(**inputs):
    x = np.ascontiguousarray(inputs["L0_attn_a_x"])
    consts = {
        "c_idf": np.eye(128, dtype=np.float32),
        "c_idb": np.eye(128, dtype=BFNP),
        "c_masku": np.triu(np.ones((128, 128), dtype=np.float32)).astype(BFNP),
        "c_sel8": np.kron(np.eye(8, dtype=np.float32),
                          np.ones((1, 128), dtype=np.float32)).astype(BFNP),
    }
    base = {}
    for k, a in inputs.items():
        if k == "L0_attn_a_x":
            continue
        a = np.ascontiguousarray(a)
        if k == "L0_attn_b_active_mask":
            a = a.view(np.uint8)
        base[k] = a
    base.update(consts)
    nc = build()
    in_maps = []
    for c in range(8):
        m = dict(base)
        m["L0_attn_a_x"] = np.ascontiguousarray(x[2 * c:2 * c + 2])
        in_maps.append(m)
    res = run_bass_kernel_spmd(nc, in_maps, core_ids=list(range(8)))
    return np.concatenate([r["out"] for r in res.results], axis=0)


# version 03 (final)
# speedup vs baseline: 1.0004x; 1.0004x over previous
import numpy as np
import ml_dtypes
import concourse.bass as bass
import concourse.mybir as mybir
from concourse.bass_utils import run_bass_kernel_spmd
from contextlib import ExitStack

F32 = mybir.dt.float32
BF16 = mybir.dt.bfloat16
U8 = mybir.dt.uint8
AF = mybir.ActivationFunctionType
ALU = mybir.AluOpType
AX = mybir.AxisListType
BFNP = ml_dtypes.bfloat16

T = 2048
D = 1024
NCH = 8
TCW = 512
NTC = T // TCW
H = 16
DH = 64
E = 8
FH = 512
FHC = FH // 128
STATE = 16
PI = float(np.pi)

STAGE = 4
DEBUG = False
GROUPS_OVERRIDE = [(0, 1, 2), (3, 4, 5), (6, 7, 8)]
PARANOID_FFN = True
LN_PARANOID_STATS = False
LN_PARANOID_ROWS = False
LN_PARANOID_APPLY = True
LN_APPLY_WAITG = True


class Plan:
    ENGS = ("t", "v", "s", "g", "y")
    NAME = {"t": "tensor", "v": "vector", "s": "scalar", "g": "gpsimd", "y": "sync"}

    def __init__(self, nc, ctx):
        self.nc = nc
        self.ops = {e: [] for e in self.ENGS}
        self.sem = {e: ctx.enter_context(nc.semaphore(f"S{e}")) for e in self.ENGS}
        self.cnt = {e: 0 for e in self.ENGS}
        self.lastw = {e: {} for e in self.ENGS}

    paranoid = False

    def put(self, e, fn, waits=(), sig=False, inc=1):
        if self.paranoid:
            waits = list(waits) + [(pe, self.cnt[pe]) for pe in self.ENGS
                                   if self.cnt[pe] > 0]
            sig = True
        w2 = []
        for tok in waits:
            if tok is None:
                continue
            pe, v = tok
            if pe == e:
                continue
            if self.lastw[e].get(pe, 0) >= v:
                continue
            self.lastw[e][pe] = v
            w2.append((pe, v))
        tok = None
        if sig:
            self.cnt[e] += inc
            tok = (e, self.cnt[e])
        self.ops[e].append((fn, w2, inc if sig else 0))
        return tok

    def emit(self, block):
        semmap = self.sem
        for e in self.ENGS:
            ops = self.ops[e]
            sem_self = self.sem[e]

            def make(ops=ops, sem_self=sem_self):
                def run(eng):
                    for fn, waits, inc in ops:
                        for pe, v in waits:
                            eng.wait_ge(semmap[pe], v)
                        ins = fn(eng)
                        if inc:
                            ins.then_inc(sem_self, inc)
                return run
            getattr(block, self.NAME[e])(make())


def _dma(out, in_, **kw):
    return lambda e: e.dma_start(out=out, in_=in_, **kw)


def _mm(out, l, r, st, sp):
    return lambda e: e.matmul(out, l, r, start=st, stop=sp)


def _tr(out, in_, ident):
    return lambda e: e.transpose(out, in_, ident)


def _act(out, in_, func, bias=0.0, scale=1.0):
    return lambda e: e.activation(out, in_, func, bias=bias, scale=scale)


def _v1(name, *args, **kw):
    return lambda e: getattr(e, name)(*args, **kw)


def build(stage=None):
    if stage is None:
        stage = STAGE
    nc = bass.Bass("TRN2", target_bir_lowering=False)
    ctx = ExitStack()

    def din(name, shape, dt=F32):
        return nc.dram_tensor(name, shape, dt, kind="ExternalInput").ap()

    x_in = din("L0_attn_a_x", [2, T, D])
    A_log = din("L0_attn_a_A_log", [STATE])
    B_in = din("L0_attn_a_B", [STATE, D])
    C_in = din("L0_attn_a_C", [D, STATE])
    D_in = din("L0_attn_a_D", [D])
    Wd_in = din("L0_attn_a_W_delta", [D, D])
    bd_in = din("L0_attn_a_b_delta", [D])
    wq_in = din("L0_attn_b_wq", [D, D])
    wk_in = din("L0_attn_b_wk", [D, D])
    wv_in = din("L0_attn_b_wv", [D, D])
    wo_in = din("L0_attn_b_wo", [D, D])
    wph_in = din("L0_attn_b_w_phase", [D, 2])
    car_in = din("L0_attn_b_carrier", [H])
    lock_in = din("L0_attn_b_lock", [H])
    bw_in = din("L0_attn_b_bandwidth", [H])
    am_in = din("L0_attn_b_active_mask", [H], U8)
    rw_in = din("L0_ffn_routing_W", [E, D])
    eb_in = din("L0_ffn_expert_bias", [E])
    sW1_in = din("L0_ffn_sW1", [D, FH])
    sb1_in = din("L0_ffn_sb1", [FH])
    sW2_in = din("L0_ffn_sW2", [FH, D])
    sb2_in = din("L0_ffn_sb2", [D])
    rW1_in = din("L0_ffn_rW1", [E, D, FH])
    rb1_in = din("L0_ffn_rb1", [E, FH])
    rW2_in = din("L0_ffn_rW2", [E, FH, D])
    rb2_in = din("L0_ffn_rb2", [E, D])
    idf_in = din("c_idf", [128, 128])
    idb_in = din("c_idb", [128, 128], BF16)
    masku_in = din("c_masku", [128, 128], BF16)
    sel8_in = din("c_sel8", [8, 1024], BF16)
    out_d = nc.dram_tensor("out", [2, T, D], F32, kind="ExternalOutput").ap()
    dbg_xn = nc.dram_tensor("dbg_xn", [128, 512], F32, kind="ExternalOutput").ap()
    dbg_dl = nc.dram_tensor("dbg_dl", [1, 2048], F32, kind="ExternalOutput").ap()
    dbg_hs = nc.dram_tensor("dbg_hs", [16, 2048], F32, kind="ExternalOutput").ap()
    dbg_da = nc.dram_tensor("dbg_da", [16, 2048], F32, kind="ExternalOutput").ap()
    dbg_db = nc.dram_tensor("dbg_db", [16, 2048], F32, kind="ExternalOutput").ap()
    dbg_bt = nc.dram_tensor("dbg_bt", [128, 128], F32, kind="ExternalOutput").ap()
    dbg_bx = nc.dram_tensor("dbg_bx", [16, 512], F32, kind="ExternalOutput").ap()
    dbg_cmb = nc.dram_tensor("dbg_cmb", [8, 512], F32, kind="ExternalOutput").ap()
    dbg_ht = nc.dram_tensor("dbg_ht", [128, 512], F32, kind="ExternalOutput").ap()
    dbg_ya = nc.dram_tensor("dbg_ya", [128, 512], F32, kind="ExternalOutput").ap()
    dbg_yb = nc.dram_tensor("dbg_yb", [128, 512], F32, kind="ExternalOutput").ap()
    dbg_acr = nc.dram_tensor("dbg_acr", [8, 512], F32, kind="ExternalOutput").ap()
    dbg_atm = nc.dram_tensor("dbg_atm", [128, 32], F32, kind="ExternalOutput").ap()
    dbg_tk = nc.dram_tensor("dbg_tk", [128, 8], F32, kind="ExternalOutput").ap()
    dbg_rw = nc.dram_tensor("dbg_rw", [128, 4096], F32, kind="ExternalOutput").ap()
    dbg_cs = nc.dram_tensor("dbg_cs", [16, 4], F32, kind="ExternalOutput").ap()
    dbg_g = nc.dram_tensor("dbg_g", [16, 512], F32, kind="ExternalOutput").ap()

    P = Plan(nc, ctx)

    def sb(name, shape, dt):
        return ctx.enter_context(nc.sbuf_tensor(name, shape, dt))

    def psum(name, shape, dt=F32):
        return ctx.enter_context(nc.psum_tensor(name, shape, dt))

    XR = sb("XR", [128, NCH * T], F32)
    AR1 = sb("AR1", [128, 16384], BF16)
    AR2 = sb("AR2", [128, 16384], BF16)
    AR3 = sb("AR3", [128, 16640], BF16)
    AR4 = sb("AR4", [128, 8192], BF16)
    AR5 = sb("AR5", [128, 4096], BF16)
    XN = sb("XN", [128, NCH * TCW], BF16)
    IDF = sb("IDF", [128, 128], F32)
    IDB = sb("IDB", [128, 128], BF16)
    MSKU = sb("MSKU", [128, 128], BF16)
    ONE1B = sb("ONE1B", [1, 128], BF16)
    ONE16F = sb("ONE16F", [1, 16], F32)
    ONE64B = sb("ONE64B", [1, 64], BF16)
    ONECB = sb("ONECB", [128, 1], BF16)
    BCOL = sb("BCOL", [128, NCH], F32)
    DCOL = sb("DCOL", [128, NCH], F32)
    SB1C = sb("SB1C", [128, FHC], F32)
    SB2C = sb("SB2C", [128, NCH], F32)
    RB1C = sb("RB1C", [128, E * FHC], F32)
    EBC = sb("EBC", [8, 1], F32)
    ALC = sb("ALC", [16, 1], F32)
    ACOL = sb("ACOL", [16, 1], F32)
    AMU = sb("AMU", [16, 1], U8)
    AMF = sb("AMF", [16, 1], F32)
    CARC = sb("CARC", [16, 1], F32)
    LOCKC = sb("LOCKC", [16, 1], F32)
    BWC = sb("BWC", [16, 1], F32)
    COSC = sb("COSC", [16, 1], F32)
    SINC = sb("SINC", [16, 1], F32)
    IBWC = sb("IBWC", [16, 1], F32)
    L1C = sb("L1C", [16, 1], F32)
    CCOL = sb("CCOL", [128, 3], F32)
    CC16 = sb("CC16", [16, 8], F32)
    AMR = sb("AMR", [1, 16], F32)
    AMH = sb("AMH", [1, 1], F32)
    INVSR = sb("INVSR", [1, TCW], BF16)
    SGBF = sb("SGBF", [64, TCW], BF16)
    INVS2 = SGBF[0:1, :]
    T16A = sb("T16A", [16, 1], F32)
    T16B = sb("T16B", [16, 1], F32)
    WPH = sb("WPH", [128, NCH * 2], BF16)
    MROW = sb("MROW", [1, TCW], BF16)
    QROW = sb("QROW", [1, TCW], BF16)
    RROW = sb("RROW", [1, TCW], BF16)
    MRROW = sb("MRROW", [1, TCW], BF16)
    TROW = sb("TROW", [1, TCW], BF16)
    XBT = [sb("XBT0", [128, TCW], BF16)] * 2
    XQT = [sb("XQT0", [128, TCW], BF16)] * 2
    TMP1 = sb("TMP1", [128, TCW], F32)
    TKM = [sb(f"TKM{i}", [128, 1], F32) for i in range(2)]
    TK8 = [sb(f"TK8{i}", [128, 8], F32) for i in range(3)]
    ATM = sb("ATM", [128, 32], F32)

    PA = psum("PA", [128, 512])
    PB = psum("PB", [128, 512])
    PC = psum("PC", [128, 512])
    PD = psum("PD", [128, 512])
    PO = psum("PO", [128, 512])
    PE_ = psum("PEx", [128, 512])
    PF = psum("PFx", [128, 512])
    PT = psum("PTx", [128, 1024], BF16)

    AR1F = AR1[:].bitcast(F32)
    AR2F = AR2[:].bitcast(F32)
    AR3F = AR3[:].bitcast(F32)
    AR4F = AR4[:].bitcast(F32)
    AR5F = AR5[:].bitcast(F32)

    bank_tok = {}

    def bfree(b):
        return bank_tok.get(id(b))

    def bset(b, tok):
        bank_tok[id(b)] = tok

    g, y, t, v, s = "g", "y", "t", "v", "s"

    def DMA(q, out, in_, waits=(), **kw):
        return P.put(q, _dma(out, in_, **kw), waits, sig=True, inc=16)

    with nc.allow_non_contiguous_dma(reason="tiny column loads"):
        tk_idf = DMA(y, IDF[:], idf_in)
        tk_idb = DMA(y, IDB[:], idb_in)
        DMA(y, MSKU[:], masku_in)
        tk_cols = DMA(y, BCOL[:], bd_in.rearrange("(c p) -> p c", p=128), allow_slow_non_contiguous=True)
        DMA(y, DCOL[:], D_in.rearrange("(c p) -> p c", p=128), allow_slow_non_contiguous=True)
        DMA(y, SB1C[:], sb1_in.rearrange("(c p) -> p c", p=128), allow_slow_non_contiguous=True)
        DMA(y, SB2C[:], sb2_in.rearrange("(c p) -> p c", p=128), allow_slow_non_contiguous=True)
        DMA(y, RB1C[:].rearrange("p (e c) -> p e c", e=E),
            rb1_in.rearrange("e (c p) -> p e c", p=128), allow_slow_non_contiguous=True)
        DMA(y, EBC[:], eb_in[:, None])
        DMA(y, ALC[:], A_log[:, None])
        DMA(y, AMU[:], am_in[:, None])
        DMA(y, CARC[:], car_in[:, None])
        DMA(y, LOCKC[:], lock_in[:, None])
        tk_small = DMA(y, BWC[:], bw_in[:, None])
    tk_wph = DMA(g, WPH[:].rearrange("p (c f) -> p c f", c=NCH),
                 wph_in.rearrange("(c p) f -> p c f", p=128))

    tk = P.put(s, _act(T16A[:], ALC[:], AF.Exp), [tk_small], sig=True)
    tk_acol = P.put(s, _act(ACOL[:], T16A[:], AF.Identity, scale=-1.0), [tk], sig=True)
    tk_amf = P.put(v, _v1("tensor_copy", AMF[:], AMU[:]), [tk_small], sig=True)
    tk_amr = P.put(t, _tr(PE_[0:1, 0:16], AMF[:], IDF[0:16, 0:16]),
                   [tk_amf, tk_idf], sig=True)
    tk_amr = P.put(v, _v1("tensor_copy", AMR[:], PE_[0:1, 0:16]), [tk_amr], sig=True)
    bset(PE_, tk_amr)
    P.put(v, _v1("memset", CC16[:, 0:1], -0.5))
    P.put(v, _v1("memset", CC16[:, 1:2], -0.25))
    P.put(v, _v1("memset", CC16[:, 2:3], -1.5))
    P.put(v, _v1("memset", CC16[:, 3:4], -1.25))
    P.put(v, _v1("memset", CC16[:, 4:5], -2 * PI))
    P.put(v, _v1("memset", CC16[:, 5:6], -1.5 * PI))
    P.put(v, _v1("memset", CC16[:, 6:7], 1.0 + 1e-6))
    tk_cc16 = P.put(v, _v1("memset", CC16[:, 7:8], 0.0), sig=True)

    def sbounce(tok):
        return P.put(v, _v1("engine_nop"), [tok], sig=True)

    def trig_path(dst, boff, breg):
        tk = P.put(s, _act(T16A[:], CARC[:], AF.Identity, scale=1.0 / (2 * PI),
                           bias=CC16[:, boff:boff + 1]), [tk_small, tk_cc16],
                   sig=True)
        tk = P.put(s, _act(T16B[:], T16A[:], AF.Sign), [tk], sig=True)
        tk = P.put(s, _act(T16A[:], CARC[:], AF.Identity, scale=1.0 / (2 * PI),
                           bias=CC16[:, boff + 2:boff + 3]), [tk], sig=True)
        tk = P.put(s, _act(dst, T16A[:], AF.Sign), [tk], sig=True)
        tk = P.put(s, _act(T16A[:], T16B[:], AF.Identity, scale=-PI,
                           bias=CC16[:, breg:breg + 1]), [tk], sig=True)
        tk = P.put(s, _act(T16B[:], dst, AF.Identity, scale=-PI,
                           bias=T16A[:]), [sbounce(tk)], sig=True)
        tk = P.put(s, _act(T16A[:], CARC[:], AF.Identity, bias=T16B[:]),
                   [sbounce(tk)], sig=True)
        tk = P.put(s, _act(dst, T16A[:], AF.Sin), [tk], sig=True)
        return tk

    tk_sinc = trig_path(SINC[:], 0, 4)
    tk_cosc = trig_path(COSC[:], 1, 5)
    tk = P.put(s, _act(T16A[:], BWC[:], AF.Ln), [tk_small, tk_cosc], sig=True)
    tk_ibw = P.put(s, _act(IBWC[:], T16A[:], AF.Exp, scale=-1.0), [tk], sig=True)
    tk_l1c = P.put(s, _act(L1C[:], LOCKC[:], AF.Identity, scale=-1.0,
                           bias=CC16[:, 6:7]), [tk_small, tk_cc16], sig=True)
    P.put(v, _v1("memset", CCOL[:, 0:1], 1e-5))
    P.put(v, _v1("memset", CCOL[:, 2:3], 1e-30))
    tk_ccol = P.put(v, _v1("memset", CCOL[:, 1:2], -12.0), sig=True)
    tk = P.put(v, _v1("memset", ONE1B[:], 1.0), sig=True)
    P.put(v, _v1("memset", ONE16F[:], 1.0))
    P.put(v, _v1("memset", ONE64B[:], 1.0))
    tk_ones = P.put(v, _v1("memset", ONECB[:], 1.0), sig=True)

    state = {"xr": None}

    def LN(b, tc, extra_waits=()):
        outer_paranoid = True
        P.paranoid = outer_paranoid and LN_PARANOID_STATS
        xr_tok = state["xr"]
        sum_tok = q_tok = None
        cast_cons = [None, None]
        for dc in range(NCH):
            pp = dc % 2
            xs = XR[:, dc * T + tc * TCW: dc * T + (tc + 1) * TCW]
            ck = P.put(v, _v1("tensor_copy", XBT[pp][:], xs),
                       [xr_tok, cast_cons[pp]] + list(extra_waits), sig=True)
            sq = P.put(s, _act(XQT[pp][:], xs, AF.Square),
                       [xr_tok, cast_cons[pp]] + list(extra_waits), sig=True)
            last = dc == NCH - 1
            sum_tok = P.put(t, _mm(PE_[0:1, 0:TCW], ONECB[:], XBT[pp][:],
                                   dc == 0, last),
                            [ck, tk_ones, bfree(PE_)], sig=last)
            q_tok = P.put(t, _mm(PF[0:1, 0:TCW], ONECB[:], XQT[pp][:],
                                 dc == 0, last),
                          [sq, bfree(PF)], sig=True)
            cast_cons[pp] = q_tok
        P.paranoid = outer_paranoid and LN_PARANOID_ROWS
        tkm = P.put(v, _v1("tensor_scalar_mul", MROW[:], PE_[0:1, 0:TCW], 1.0 / D),
                    [sum_tok], sig=True)
        bset(PE_, tkm)
        tkq = P.put(v, _v1("tensor_scalar_mul", QROW[:], PF[0:1, 0:TCW], 1.0 / D),
                    [q_tok], sig=True)
        bset(PF, tkq)
        tk1 = P.put(v, _v1("tensor_mul", TROW[:], MROW[:], MROW[:]), [tkm], sig=True)
        tk2 = P.put(v, _v1("tensor_sub", QROW[:], QROW[:], TROW[:]), [tk1], sig=True)
        tk3 = P.put(s, _act(TROW[:], QROW[:], AF.Sqrt, bias=CCOL[0:1, 0:1]), [tk2], sig=True)
        tk4 = P.put(v, _v1("reciprocal", RROW[:], TROW[:]), [tk3], sig=True)
        tk5 = P.put(v, _v1("tensor_mul", MRROW[:], MROW[:], RROW[:]), [tk4], sig=True)
        mmr = P.put(t, _mm(PC[:, 0:TCW], ONE1B[:], RROW[:], True, True),
                    [tk4, bfree(PC)], sig=True)
        mmm = P.put(t, _mm(PD[:, 0:TCW], ONE1B[:], MRROW[:], True, True),
                    [tk5, bfree(PD)], sig=True)
        P.paranoid = outer_paranoid and LN_PARANOID_APPLY
        xn_tok = None
        for dc in range(NCH):
            xs = XR[:, dc * T + tc * TCW: dc * T + (tc + 1) * TCW]
            wg = [("g", P.cnt["g"])] if (outer_paranoid and LN_APPLY_WAITG
                                         and P.cnt["g"] > 0) else []
            a1 = P.put(v, _v1("tensor_mul", TMP1[:], xs, PC[:, 0:TCW]),
                       [mmr] + wg, sig=False)
            xn_tok = P.put(v, _v1("tensor_sub", XN[:, dc * TCW:(dc + 1) * TCW],
                                  TMP1[:], PD[:, 0:TCW]),
                           [mmm], sig=(dc == NCH - 1))
        bset(PC, xn_tok)
        bset(PD, xn_tok)
        P.paranoid = outer_paranoid
        return xn_tok

    def io_in(b):
        STG = [AR5F[:, 0:1024], AR5F[:, 1024:2048]]
        dma_tok = [None, None]
        cons_tok = state.get("stg_cons", [None, None])
        xr_view = XR[:].rearrange("p (c t) -> p c t", c=NCH)
        last = None
        for tt in range(T // 128):
            pp = tt % 2
            dma_tok[pp] = DMA(y, STG[pp], x_in[b, tt * 128:(tt + 1) * 128, :],
                              waits=[cons_tok[pp]])
            for half in range(2):
                bank = PE_ if half == 0 else PF
                trt = None
                for j in range(4):
                    dc = half * 4 + j
                    trt = P.put(t, _tr(bank[:, j * 128:(j + 1) * 128],
                                       STG[pp][:, dc * 128:(dc + 1) * 128], IDF[:]),
                                [dma_tok[pp], tk_idf, bfree(bank)],
                                sig=(j == 3))
                cp = P.put(v, _v1(
                    "tensor_copy",
                    xr_view[:, half * 4:(half + 1) * 4, tt * 128:(tt + 1) * 128],
                    bank[:, 0:512].rearrange("p (c q) -> p c q", c=4)),
                    [trt], sig=True)
                bset(bank, cp)
                cons_tok[pp] = cp
                last = cp
        state["xr"] = last

    def io_out(b):
        STG = [AR5F[:, 0:1024], AR5F[:, 1024:2048]]
        xr_view = XR[:].rearrange("p (c t) -> p c t", c=NCH)
        dma_cons = [None, None]
        final_toks = []
        for tt in range(T // 128):
            pp = tt % 2
            cp = None
            for half in range(2):
                bank = PE_ if half == 0 else PF
                trt = None
                for j in range(4):
                    dc = half * 4 + j
                    trt = P.put(t, _tr(bank[:, j * 128:(j + 1) * 128],
                                       XR[:, dc * T + tt * 128: dc * T + (tt + 1) * 128],
                                       IDF[:]),
                                [state["xr"], bfree(bank), dma_cons[pp]],
                                sig=(j == 3))
                cp = P.put(v, _v1(
                    "tensor_copy",
                    STG[pp][:, half * 512:(half + 1) * 512]
                    .rearrange("p (c q) -> p c q", c=4),
                    bank[:, 0:512].rearrange("p (c q) -> p c q", c=4)),
                    [trt], sig=True)
                bset(bank, cp)
            dtok = DMA(y, out_d[b, tt * 128:(tt + 1) * 128, :], STG[pp], waits=[cp])
            dma_cons[pp] = dtok
            final_toks.append(dtok)
        state["stg_cons"] = list(dma_cons)
        return final_toks

    def ssm(b):
        dA = AR1F[0:16, 0:2048]
        HS = AR1F[0:16, 2048:4096]
        dBxB = AR1[0:16, 8192:10240]
        HSB = AR1[0:16, 10240:12288]
        DELT = AR1[0:1, 12288:14336]
        EZ = [AR1F[:, 7168:7680], AR1F[:, 7680:8192]]
        SPT = [AR4[:, 0:512], AR4[:, 512:1024]]
        WdN = [AR2[:, 0:1024], AR2[:, 1024:2048]]
        WdT = AR2[:, 2048:10240]
        BTv = AR2[:, 10240:10368]
        CTv = AR2[0:16, 10368:11392]
        BN = AR2[0:16, 11392:12416]
        CN = AR2[:, 12416:12432]
        DBS = AR2F[0:16, 6272:6784]

        tkB = DMA(g, BN, B_in, waits=[state["xr"]])
        trt = None
        for c in range(NCH):
            trt = P.put(t, _tr(PT[:, c * 16:(c + 1) * 16],
                               BN[:, c * 128:(c + 1) * 128], IDB[0:16, 0:16]),
                        [tkB, tk_idb, bfree(PT)], sig=(c == NCH - 1))
        tkBT = P.put(v, _v1("tensor_copy", BTv, PT[:, 0:128]), [trt], sig=True)
        bset(PT, tkBT)
        if DEBUG and b == 0:
            cbt = P.put(v, _v1("tensor_copy", TMP1[:, 0:128], BTv), [tkBT], sig=True)
            DMA(y, dbg_bt, TMP1[:, 0:128], waits=[cbt])
        ct_toks = []
        prev_cn = tkBT
        for c in range(NCH):
            tkC = DMA(g, CN, C_in[c * 128:(c + 1) * 128, :], waits=[prev_cn])
            trt = P.put(t, _tr(PT[0:16, c * 128:(c + 1) * 128], CN, IDB[:]),
                        [tkC, bfree(PT)], sig=True)
            prev_cn = trt
            ct_toks.append(trt)
        tkCT = P.put(v, _v1("tensor_copy", CTv, PT[0:16, 0:1024]), ct_toks, sig=True)
        bset(PT, tkCT)
        wd_evac = None
        for ec in range(NCH):
            pp = ec % 2
            tkW = DMA(g, WdN[pp][:], Wd_in[ec * 128:(ec + 1) * 128, :],
                      waits=[wd_evac, state["xr"]])
            trt = None
            for dc in range(NCH):
                trt = P.put(t, _tr(PT[:, dc * 128:(dc + 1) * 128],
                                   WdN[pp][:, dc * 128:(dc + 1) * 128], IDB[:]),
                            [tkW, bfree(PT)], sig=(dc == NCH - 1))
            wd_evac = P.put(v, _v1(
                "tensor_copy",
                WdT.rearrange("p (d e) -> p d e", d=NCH)[:, :, ec * 128:(ec + 1) * 128],
                PT[:, 0:1024].rearrange("p (d q) -> p d q", d=NCH)),
                [trt], sig=True)
            bset(PT, wd_evac)

        for tc in range(NTC):
            xn_tok = LN(b, tc, extra_waits=[wd_evac] if tc == 0 else [])
            P.paranoid = False
            if DEBUG and b == 0 and tc == 0:
                cpdbg = P.put(v, _v1("tensor_copy", TMP1[:], XN[:, 0:TCW]),
                              [xn_tok], sig=True)
                DMA(y, dbg_xn, TMP1[:], waits=[cpdbg])
            ds_tok = None
            ez_cons = [None, None]
            for ec in range(NCH):
                bankz = PA if ec % 2 == 0 else PB
                mmt = None
                for dc in range(NCH):
                    l = WdT[:, (dc * NCH + ec) * 128:(dc * NCH + ec + 1) * 128]
                    mmt = P.put(t, _mm(bankz[:, 0:TCW], l,
                                       XN[:, dc * TCW:(dc + 1) * TCW],
                                       dc == 0, dc == NCH - 1),
                                [xn_tok, wd_evac, bfree(bankz)],
                                sig=(dc == NCH - 1))
                pp = ec % 2
                e1 = P.put(s, _act(EZ[pp], bankz[:, 0:TCW], AF.Exp,
                                   bias=BCOL[:, ec:ec + 1]),
                           [mmt, tk_cols, ez_cons[pp]], sig=True)
                bset(bankz, e1)
                e2 = P.put(s, _act(SPT[pp][:], EZ[pp], AF.Ln, bias=1.0),
                           [e1], sig=True)
                ds_tok = P.put(t, _mm(PE_[0:1, 0:TCW], ONECB[:], SPT[pp][:],
                                      ec == 0, ec == NCH - 1),
                               [e2, bfree(PE_)], sig=(ec == NCH - 1))
                ez_cons[pp] = ds_tok
            dl_tok = P.put(v, _v1("tensor_scalar_mul",
                                  DELT[:, tc * TCW:(tc + 1) * TCW],
                                  PE_[0:1, 0:TCW], 1.0 / D),
                           [ds_tok], sig=True)
            bset(PE_, dl_tok)
            for dc in range(NCH):
                xs = XR[:, dc * T + tc * TCW: dc * T + (tc + 1) * TCW]
                tk_x = P.put(v, _v1("scalar_tensor_tensor", xs,
                                    XN[:, dc * TCW:(dc + 1) * TCW],
                                    DCOL[:, dc:dc + 1], xs, ALU.mult, ALU.add),
                             [xn_tok], sig=(dc == NCH - 1))
            state["xr"] = tk_x
            bx_tok = None
            for dc in range(NCH):
                bx_tok = P.put(t, _mm(PC[0:16, 0:TCW],
                                      BTv.rearrange("p (c n) -> p c n", c=NCH)[:, dc, :],
                                      XN[:, dc * TCW:(dc + 1) * TCW],
                                      dc == 0, dc == NCH - 1),
                               [xn_tok, tkBT, bfree(PC)], sig=(dc == NCH - 1))
            db_tok = P.put(t, _mm(PD[0:16, 0:TCW], ONE1B[0:1, 0:16],
                                  DELT[:, tc * TCW:(tc + 1) * TCW], True, True),
                           [dl_tok, bfree(PD)], sig=True)
            if DEBUG and b == 0 and tc == 0:
                cbx = P.put(v, _v1("tensor_copy", AR3F[0:16, 3072:3584],
                                   PC[0:16, 0:TCW]), [bx_tok], sig=True)
                DMA(y, dbg_bx, AR3F[0:16, 3072:3584], waits=[cbx])
            cp_tok = P.put(v, _v1("tensor_copy", DBS, PD[0:16, 0:TCW]),
                           [db_tok], sig=True)
            bset(PD, cp_tok)
            t1 = P.put(v, _v1("tensor_scalar_mul", EZ[0][0:16, 0:TCW], DBS, ACOL[:]),
                       [cp_tok, tk_acol], sig=True)
            tda = P.put(s, _act(dA[:, tc * TCW:(tc + 1) * TCW], EZ[0][0:16, 0:TCW],
                                AF.Exp), [t1], sig=True)
            tdb = P.put(v, _v1("tensor_mul", dBxB[:, tc * TCW:(tc + 1) * TCW],
                               PC[0:16, 0:TCW], DBS), [bx_tok, cp_tok], sig=True)
            bset(PC, tdb)
            if tc == NTC - 1:
                scan_in = [tda, tdb]
        tk_scan = P.put(v, _v1("tensor_tensor_scan", HS, dA, dBxB, 0.0,
                               ALU.mult, ALU.add), scan_in, sig=True)
        tk_hsb = P.put(v, _v1("tensor_copy", HSB, HS), [tk_scan], sig=True)
        if DEBUG and b == 0:
            DMA(y, dbg_hs, HS, waits=[tk_scan])
            DMA(y, dbg_da, dA, waits=[tk_scan])
            cdb1 = P.put(v, _v1("tensor_copy", AR3F[0:16, 4096:6144][0:1, 0:2048],
                                DELT), [tk_scan], sig=True)
            DMA(y, dbg_dl, AR3F[0:16, 4096:6144][0:1, 0:2048], waits=[cdb1])
            cdb2 = P.put(v, _v1("tensor_copy", AR3F[0:16, 0:2048], dBxB),
                         [tk_scan], sig=True)
            DMA(y, dbg_db, AR3F[0:16, 0:2048], waits=[cdb2])
        xr_last = None
        for tc in range(NTC):
            for dc in range(NCH):
                bank = PA if (tc * NCH + dc) % 2 == 0 else PB
                mmt = P.put(t, _mm(bank[:, 0:TCW],
                                   CTv.rearrange("p (c q) -> p c q", c=NCH)[:, dc, :],
                                   HSB[:, tc * TCW:(tc + 1) * TCW], True, True),
                            [tk_hsb, tkCT, bfree(bank)], sig=True)
                xs = XR[:, dc * T + tc * TCW: dc * T + (tc + 1) * TCW]
                xr_last = P.put(v, _v1("tensor_add", xs, xs, bank[:, 0:TCW]),
                                [mmt, state["xr"]], sig=True)
                bset(bank, xr_last)
        state["xr"] = xr_last

    GROUPS = GROUPS_OVERRIDE

    def ffn_unit_w(u):
        if u == 0:
            return sW1_in, sW2_in
        return rW1_in[u - 1], rW2_in[u - 1]

    def ffn(b, call):
        RW1 = AR1[:, 0:12288]
        RW2 = AR2[:, 0:12288]
        XNH = [AR1[:, 12288 + dc * 1024: 12288 + (dc + 1) * 1024] if dc < 4
               else AR2[:, 12288 + (dc - 4) * 1024: 12288 + (dc - 3) * 1024]
               for dc in range(NCH)]
        HT = AR3[:, 0:6144]
        YACC = [AR4F[:, 0:4096], AR3F[:, 3072:7168]]
        RTv = AR5[:, 0:64]
        RB2v = AR5[0:8, 64:1088]
        RWN8 = AR5[0:8, 1088:2112]
        CMB = AR5[0:8, 2112:3136]
        ACR = [AR3F[0:8, 7168:7680], AR3F[0:8, 7680:8192]]

        ffn_free = state["xr"]
        P.paranoid = PARANOID_FFN
        tk_rwn = DMA(g, RWN8, rw_in, waits=[ffn_free])
        tk_rb2 = DMA(g, RB2v, rb2_in, waits=[ffn_free])
        trt = None
        for c in range(NCH):
            trt = P.put(t, _tr(PT[:, c * 8:(c + 1) * 8],
                               RWN8[:, c * 128:(c + 1) * 128], IDB[0:8, 0:8]),
                        [tk_rwn, bfree(PT)], sig=(c == NCH - 1))
        tk_rt = P.put(v, _v1("tensor_copy", RTv, PT[:, 0:64]), [trt], sig=True)
        bset(PT, tk_rt)
        SEL8 = AR5[0:8, 1088:2112]
        tk_sel = DMA(y, SEL8, sel8_in, waits=[tk_rt])

        ng = len(GROUPS)
        w1tok = {}
        w2tok = {}

        def load_w1(gi, waits):
            for si, u in enumerate(GROUPS[gi]):
                w1, _ = ffn_unit_w(u)
                tok = DMA(g, RW1[:, si * 4096:(si + 1) * 4096]
                          .rearrange("p (c f) -> p c f", c=NCH),
                          w1.rearrange("(c p) f -> p c f", p=128), waits=waits)
            w1tok[gi] = tok

        def load_w2(gi, waits):
            for si, u in enumerate(GROUPS[gi]):
                _, w2 = ffn_unit_w(u)
                tok = DMA(g, RW2[:, si * 4096:(si + 1) * 4096]
                          .rearrange("p (c f) -> p c f", c=FHC),
                          w2.rearrange("(c p) f -> p c f", p=128), waits=waits)
            w2tok[gi] = tok

        load_w1(0, [ffn_free])
        load_w2(0, [ffn_free])

        xr_last = state["xr"]
        for half in range(2):
            P.paranoid = PARANOID_FFN
            xn_toks = {}
            cmb_toks = {}
            for tci in range(2):
                tc = half * 2 + tci
                P.paranoid = PARANOID_FFN
                xn_tok = LN(b, tc)
                P.paranoid = False
                cps = None
                for dc in range(NCH):
                    cps = P.put(v, _v1(
                        "tensor_copy", XNH[dc][:, tci * 512:(tci + 1) * 512],
                        XN[:, dc * TCW:(dc + 1) * TCW]),
                        [xn_tok, state.get("h1_last_mm")], sig=(dc == NCH - 1))
                xn_toks[tci] = cps
                lg = None
                for dc in range(NCH):
                    lg = P.put(t, _mm(PF[0:8, 0:TCW],
                                      RTv.rearrange("p (c e) -> p c e", c=NCH)[:, dc, :],
                                      XN[:, dc * TCW:(dc + 1) * TCW],
                                      dc == 0, dc == NCH - 1),
                               [xn_tok, tk_rt, bfree(PF)], sig=(dc == NCH - 1))
                a1 = P.put(s, _act(ACR[0], PF[0:8, 0:TCW], AF.Exp, bias=EBC[:]),
                           [lg], sig=True)
                bset(PF, a1)
                a2 = P.put(s, _act(ACR[1], ACR[0], AF.Ln, bias=1.0), [a1], sig=True)
                a3 = P.put(s, _act(ACR[0], ACR[1], AF.Sqrt), [a2], sig=True)
                if DEBUG and b == 0 and call == 0 and tc == 0:
                    DMA(y, dbg_acr, ACR[0], waits=[a3])
                trt = None
                for j in range(4):
                    trt = P.put(t, _tr(PE_[:, j * 8:(j + 1) * 8],
                                       ACR[0][:, j * 128:(j + 1) * 128], IDF[0:8, 0:8]),
                                [a3, bfree(PE_)], sig=(j == 3))
                cpa = P.put(v, _v1("tensor_copy", ATM[:], PE_[:, 0:32]), [trt],
                            sig=True)
                bset(PE_, cpa)
                if DEBUG and b == 0 and call == 0 and tc == 0:
                    DMA(y, dbg_atm, ATM[:], waits=[cpa])
                cb_tok = None
                prev_tmk = state.get("tmk")
                prev_trb = state.get("trb")
                for j in range(4):
                    at = ATM[:, j * 8:(j + 1) * 8]
                    tmx = P.put(v, _v1("max", TK8[0][:], at), [cpa, prev_tmk],
                                sig=True)
                    tm2 = P.put(s, _act(TKM[0][:], TK8[0][:, 1:2], AF.Identity),
                                [tmx], sig=True)
                    td = P.put(v, _v1("tensor_scalar_sub", TK8[1][:], at,
                                      TKM[0][:]), [tm2, prev_trb], sig=True)
                    tr_ = P.put(s, _act(TK8[2][:], TK8[1][:], AF.Relu,
                                        scale=1e12, bias=1.0), [td], sig=True)
                    P.put(s, _act(TK8[0][:], TK8[2][:], AF.Relu,
                                  scale=-1.0, bias=1.0), [])
                    tmk = P.put(s, _act(TK8[2][:], TK8[0][:], AF.Identity,
                                        scale=-1.0, bias=1.0), [], sig=True)
                    prev_tmk = tmk
                    state["tmk"] = tmk
                    tk1 = P.put(v, _v1("tensor_mul", TK8[1][:], at, TK8[2][:]),
                                [tmk], sig=True)
                    trb = P.put(t, _tr(PF[0:8, j * 128:(j + 1) * 128],
                                       TK8[1][:], IDF[:]),
                                [tk1, bfree(PF)], sig=True)
                    prev_trb = trb
                    state["trb"] = trb
                    cb_tok = P.put(v, _v1(
                        "tensor_copy",
                        CMB[:, tci * 512 + j * 128: tci * 512 + (j + 1) * 128],
                        PF[0:8, j * 128:(j + 1) * 128]), [trb], sig=True)
                    bset(PF, cb_tok)
                    if DEBUG and b == 0 and call == 0 and tc == 0 and j == 0:
                        DMA(y, dbg_tk, TK8[1][:], waits=[tk1])
                cmb_toks[tci] = cb_tok
                if DEBUG and b == 0 and call == 0 and tc == 0:
                    DMA(g, dbg_cmb, CMB[0:8, 0:512], waits=[cb_tok])

            for gi in range(ng):
                gl = GROUPS[gi]
                h1tok = {}
                last_h1_scale = None
                for tci in range(2):
                    tc = half * 2 + tci
                    for si, u in enumerate(gl):
                        if u > 0:
                            cbt = P.put(t, _mm(PE_[:, 0:TCW],
                                               SEL8[0:8, (u - 1) * 128:u * 128],
                                               CMB[0:8, tci * 512:(tci + 1) * 512],
                                               True, True),
                                        [cmb_toks[tci], tk_sel, bfree(PE_)], sig=True)
                        for hc in range(FHC):
                            bank = PA if (si * FHC + hc) % 2 == 0 else PB
                            mmt = None
                            for dc in range(NCH):
                                l = RW1[:, si * 4096 + dc * 512 + hc * 128:
                                        si * 4096 + dc * 512 + (hc + 1) * 128]
                                mmt = P.put(t, _mm(bank[:, 0:TCW], l,
                                                   XNH[dc][:, tci * 512:(tci + 1) * 512],
                                                   dc == 0, dc == NCH - 1),
                                            [xn_toks[tci], w1tok[gi], bfree(bank)],
                                            sig=(dc == NCH - 1))
                            state["h1_last_mm"] = mmt
                            hslot = HT[:, si * 2048 + hc * 512:
                                       si * 2048 + (hc + 1) * 512]
                            bias = SB1C[:, hc:hc + 1] if u == 0 else \
                                RB1C[:, (u - 1) * FHC + hc:(u - 1) * FHC + hc + 1]
                            sil = P.put(s, _act(hslot, bank[:, 0:TCW], AF.Silu,
                                                bias=bias), [mmt, tk_cols], sig=True)
                            bset(bank, sil)
                            if u > 0:
                                sil = P.put(v, _v1("tensor_mul", hslot, hslot,
                                                   PE_[:, 0:TCW]), [sil, cbt],
                                            sig=True)
                            h1tok[(si, hc)] = sil
                            last_h1_scale = sil
                            if (DEBUG and b == 0 and call == 0 and half == 0
                                    and tci == 0 and gi == 0 and si == 1
                                    and hc == 0):
                                DMA(g, dbg_ht, hslot, waits=[sil])
                                DMA(g, dbg_rw, RW1[:, 4096:8192], waits=[sil])
                        if u > 0:
                            bset(PE_, last_h1_scale)
                    for fc in range(NCH):
                        bank = PC if fc % 2 == 0 else PD
                        mms = []
                        for si2 in range(len(gl)):
                            for hc in range(FHC):
                                l = RW2[:, si2 * 4096 + hc * 1024 + fc * 128:
                                        si2 * 4096 + hc * 1024 + (fc + 1) * 128]
                                hslot = HT[:, si2 * 2048 + hc * 512:
                                           si2 * 2048 + (hc + 1) * 512]
                                mms.append((l, hslot, [h1tok[(si2, hc)],
                                                       w2tok[gi]]))
                        if gi == 0:
                            mms.append((RB2v[:, fc * 128:(fc + 1) * 128],
                                        CMB[:, tci * 512:(tci + 1) * 512],
                                        [tk_rb2, cmb_toks[tci]]))
                        mmt = None
                        for i, (l, r, ew) in enumerate(mms):
                            mmt = P.put(t, _mm(bank[:, 0:TCW], l, r,
                                               i == 0, i == len(mms) - 1),
                                        list(ew) + [bfree(bank)],
                                        sig=(i == len(mms) - 1))
                        ya = YACC[fc // 4][:, (fc % 4) * 1024 + tci * 512:
                                           (fc % 4) * 1024 + (tci + 1) * 512]
                        xs = XR[:, fc * T + tc * TCW: fc * T + (tc + 1) * TCW]
                        if gi == 0:
                            ev = P.put(s, _act(ya, bank[:, 0:TCW], AF.Identity,
                                               bias=SB2C[:, fc:fc + 1]),
                                       [mmt, state.get(("yacons", fc, tci))],
                                       sig=True)
                        elif gi == 1:
                            ev = P.put(v, _v1("tensor_add", ya, ya, bank[:, 0:TCW]),
                                       [mmt, state.get(("ya", fc, tci))], sig=True)
                        else:
                            e1 = P.put(v, _v1("tensor_add", TMP1[:], ya,
                                              bank[:, 0:TCW]),
                                       [mmt, state.get(("ya", fc, tci))], sig=False)
                            ev = P.put(v, _v1("tensor_add", xs, xs, TMP1[:]),
                                       [xr_last], sig=True)
                            xr_last = ev
                            state[("yacons", fc, tci)] = ev
                        if (DEBUG and b == 0 and call == 0 and half == 0
                                and tci == 0 and fc == 0 and gi <= 1):
                            DMA(y, dbg_ya if gi == 0 else dbg_yb, ya,
                                waits=[ev])
                        state[("ya", fc, tci)] = ev
                        bset(bank, ev)
                if not (half == 1 and gi == ng - 1):
                    nxt = (gi + 1) % ng
                    load_w1(nxt, [state["h1_last_mm"]])
                    load_w2(nxt, [state[("ya", NCH - 1, 1)]])
        P.paranoid = False
        state["xr"] = xr_last
        state["ffn_free"] = xr_last


    def attn(b):
        QO = AR1
        KF = AR2
        VT = AR3[:, 0:16384]
        QPAD = AR4[:, 0:2048]
        PTB = [AR4[:, 2048:2560], AR4[:, 2560:3072]]
        GTC = AR4[0:16, 3072:3584]
        GT12 = AR4F[:, 1792:2048]
        PH0R = AR4F[0:1, 2048:2560]
        PH1R = AR4F[0:1, 2560:3072]
        CDv = AR4F[0:16, 3072:3584]
        CDBv = AR4F[0:16, 3584:4096]
        RAv = CDv[0:1, :]
        RBv = CDBv[0:1, :]
        WSL = [AR5[:, 0:2048], AR5[:, 2048:4096]]
        arena_free = state["xr"]
        slot_cons = [arena_free, arena_free]
        slot_i = 0
        proj_toks = []

        def wslice(win, q4):
            return win[:, q4 * 256:(q4 + 1) * 256].rearrange(
                "(c p) f -> p c f", p=128)

        for tc in range(NTC):
            xn_tok = LN(b, tc, extra_waits=[arena_free] if tc == 0 else [])
            for wi, (w_in, dst) in enumerate([(wq_in, QO), (wk_in, KF)]):
                for q4 in range(4):
                    sl = slot_i % 2
                    slot_i += 1
                    wtok = DMA(g, WSL[sl][:].rearrange("p (c f) -> p c f", c=NCH),
                               wslice(w_in, q4), waits=[slot_cons[sl]])
                    for fc2 in range(2):
                        fcg = q4 * 2 + fc2
                        bank = PA if fcg % 2 == 0 else PB
                        mmt = None
                        for dc in range(NCH):
                            l = WSL[sl][:, dc * 256 + fc2 * 128:
                                        dc * 256 + (fc2 + 1) * 128]
                            mmt = P.put(t, _mm(bank[:, 0:TCW], l,
                                               XN[:, dc * TCW:(dc + 1) * TCW],
                                               dc == 0, dc == NCH - 1),
                                        [xn_tok, wtok, bfree(bank)],
                                        sig=(dc == NCH - 1))
                        slot_cons[sl] = mmt
                        ev = P.put(v, _v1(
                            "tensor_copy",
                            dst[:, fcg * T + tc * TCW: fcg * T + (tc + 1) * TCW],
                            bank[:, 0:TCW]), [mmt], sig=True)
                        bset(bank, ev)
                        proj_toks.append(ev)
            for q4 in range(4):
                sl = slot_i % 2
                slot_i += 1
                wtok = DMA(g, WSL[sl][:].rearrange("p (c f) -> p c f", c=NCH),
                           wslice(wv_in, q4), waits=[slot_cons[sl]])
                for tt in range(4):
                    bank = PA if tt % 2 == 0 else PB
                    mmt = None
                    for dc in range(NCH):
                        l = XN[:, dc * TCW + tt * 128: dc * TCW + (tt + 1) * 128]
                        mmt = P.put(t, _mm(bank[:, 0:256], l,
                                           WSL[sl][:, dc * 256:(dc + 1) * 256],
                                           dc == 0, dc == NCH - 1),
                                    [xn_tok, wtok, bfree(bank)],
                                    sig=(dc == NCH - 1))
                    slot_cons[sl] = mmt
                    kt = tc * 4 + tt
                    ev = P.put(v, _v1(
                        "tensor_copy",
                        VT[:, kt * 1024: (kt + 1) * 1024]
                        .rearrange("p (h c) -> p h c", c=64)[:, q4 * 4:(q4 + 1) * 4, :],
                        bank[:, 0:256].rearrange("p (h c) -> p h c", c=64)),
                        [mmt], sig=True)
                    bset(bank, ev)
                    proj_toks.append(ev)
            ph0 = ph1 = None
            for dc in range(NCH):
                ph0 = P.put(t, _mm(PE_[0:1, 0:TCW], WPH[:, dc * 2:dc * 2 + 1],
                                   XN[:, dc * TCW:(dc + 1) * TCW],
                                   dc == 0, dc == NCH - 1),
                            [xn_tok, tk_wph, bfree(PE_)], sig=(dc == NCH - 1))
                ph1 = P.put(t, _mm(PF[0:1, 0:TCW], WPH[:, dc * 2 + 1:dc * 2 + 2],
                                   XN[:, dc * TCW:(dc + 1) * TCW],
                                   dc == 0, dc == NCH - 1),
                            [bfree(PF)], sig=(dc == NCH - 1))
            t0 = P.put(v, _v1("tensor_copy", PH0R, PE_[0:1, 0:TCW]), [ph0], sig=True)
            bset(PE_, t0)
            t1 = P.put(v, _v1("tensor_copy", PH1R, PF[0:1, 0:TCW]), [ph1], sig=True)
            bset(PF, t1)
            t2 = P.put(v, _v1("tensor_mul", RAv, PH0R, PH0R), [t0])
            t2 = P.put(v, _v1("tensor_mul", RBv, PH1R, PH1R), [t1])
            t2 = P.put(v, _v1("tensor_add", RAv, RAv, RBv), [], sig=True)
            t3 = P.put(s, _act(RBv, RAv, AF.Sqrt, bias=CCOL[0:1, 2:3]), [t2], sig=True)
            t4 = P.put(v, _v1("reciprocal", RAv, RBv), [t3], sig=True)
            tu = P.put(v, _v1("tensor_mul", PH1R, PH1R, RAv), [t4], sig=True)
            tv = P.put(v, _v1("tensor_mul", PH0R, PH0R, RAv), [t4], sig=True)
            ub = P.put(t, _mm(PE_[0:16, 0:TCW], ONE16F[:], PH1R, True, True),
                       [tu, bfree(PE_)], sig=True)
            vb = P.put(t, _mm(PF[0:16, 0:TCW], ONE16F[:], PH0R, True, True),
                       [tv, bfree(PF)], sig=True)
            c1 = P.put(v, _v1("tensor_scalar_mul", CDv, PE_[0:16, 0:TCW],
                              COSC[:]), [ub], sig=True)
            bset(PE_, c1)
            c2 = P.put(v, _v1("scalar_tensor_tensor", CDv, PF[0:16, 0:TCW],
                              SINC[:], CDv, ALU.mult, ALU.add), [c1], sig=True)
            bset(PF, c2)
            c3 = P.put(v, _v1("tensor_scalar_add", CDBv, CDv, -1.0), [c2])
            c4 = P.put(v, _v1("tensor_scalar_mul", CDv, CDBv, IBWC[:]), [],
                       sig=True)
            c5 = P.put(s, _act(CDBv, CDv, AF.Exp), [c4], sig=True)
            c6 = P.put(v, _v1("tensor_scalar_mul", CDv, CDBv, LOCKC[:]), [c5])
            c7 = P.put(v, _v1("tensor_scalar_add", CDBv, CDv, L1C[:]), [],
                       sig=True)
            c8 = P.put(s, _act(GTC, CDBv, AF.Ln), [c7], sig=True)
            if DEBUG and b == 0 and tc == 0:
                DMA(g, dbg_g, GTC, waits=[c8])
            trt = None
            for j in range(4):
                trt = P.put(t, _tr(PT[:, j * 16:(j + 1) * 16],
                                   GTC[:, j * 128:(j + 1) * 128],
                                   IDB[0:16, 0:16]),
                            [c8, bfree(PT)], sig=(j == 3))
            gevs = None
            for j in range(4):
                kt = tc * 4 + j
                gevs = P.put(s, _act(GT12[:, kt * 16:(kt + 1) * 16],
                                     PT[:, j * 16:(j + 1) * 16], AF.Identity,
                                     bias=CCOL[:, 1:2]), [trt],
                             sig=(j == 3))
            bset(PT, gevs)
            proj_toks.append(gevs)

        pd = proj_toks
        last_smm = None
        last_sg = None
        amh_cons = None
        pt_cons = [None, None]
        for pc in range(NCH):
            for par in range(2):
                h = 2 * pc + par
                amh_tok = P.put(s, _act(AMH[:], AMR[0:1, h:h + 1], AF.Identity),
                                [tk_amr, amh_cons], sig=True)
                mz = P.put(v, _v1("memset",
                                  QPAD[(1 - par) * 64:(2 - par) * 64, :], 0.0),
                           [last_smm] + pd, sig=True)
                qp = P.put(v, _v1("tensor_copy",
                                  QPAD[par * 64:(par + 1) * 64, :],
                                  QO[par * 64:(par + 1) * 64,
                                     pc * T:(pc + 1) * T]),
                           [mz], sig=True)
                for qc in range(NTC):
                    obank = PO if qc % 2 == 0 else PC
                    sbank = PF if qc % 2 == 0 else PD
                    kmax = 4 * qc + 3
                    for ki in range(kmax + 1):
                        pp = ki % 2
                        sb_ = PA if pp == 0 else PB
                        diag = ki - 4 * qc
                        colsA = max(diag, 0) * 128
                        smm = P.put(t, _mm(sb_[:, colsA:512],
                                           KF[:, pc * T + ki * 128:
                                              pc * T + (ki + 1) * 128],
                                           QPAD[:, qc * TCW + colsA:
                                                (qc + 1) * TCW],
                                           True, True),
                                    [qp, bfree(sb_)], sig=True)
                        last_smm = smm
                        ex = P.put(s, _act(PTB[pp][:, colsA:512],
                                           sb_[:, colsA:512], AF.Exp, scale=0.125,
                                           bias=GT12[:, ki * 16 + h:
                                                     ki * 16 + h + 1]),
                                   [smm, pt_cons[pp]], sig=True)
                        bset(sb_, ex)
                        dtok = ex
                        if diag >= 0:
                            dtok = P.put(v, _v1("tensor_mul",
                                               PTB[pp][:, colsA:colsA + 128],
                                               PTB[pp][:, colsA:colsA + 128],
                                               MSKU[:]), [ex], sig=True)
                            if colsA > 0:
                                dtok = P.put(v, _v1("memset",
                                                    PTB[pp][:, 0:colsA], 0.0),
                                             [ex], sig=True)
                        av = P.put(t, _mm(obank[0:64, 0:512],
                                          VT[:, (ki * 16 + h) * 64:
                                             (ki * 16 + h + 1) * 64],
                                          PTB[pp][:, 0:512],
                                          ki == 0, ki == kmax),
                                   [dtok, bfree(obank)], sig=False)
                        sg_ = P.put(t, _mm(sbank[0:1, 0:512], ONECB[:],
                                           PTB[pp][:, 0:512],
                                           ki == 0, ki == kmax),
                                    [bfree(sbank)], sig=True)
                        pt_cons[pp] = sg_
                        last_sg = sg_
                    rc = P.put(v, _v1("reciprocal", INVSR[:],
                                      sbank[0:1, 0:512]), [last_sg], sig=True)
                    bset(sbank, rc)
                    ms = P.put(v, _v1("tensor_scalar_mul", INVS2, INVSR[:],
                                      AMH[:]), [rc, amh_tok], sig=True)
                    amh_cons = ms
                    bc = P.put(t, _mm(PE_[0:64, 0:512], ONE64B[:],
                                      INVS2, True, True),
                               [ms, bfree(PE_)], sig=True)
                    cp = P.put(v, _v1("tensor_copy", SGBF[:], PE_[0:64, 0:512]),
                               [bc], sig=True)
                    bset(PE_, cp)
                    dst = (QO if par == 0 else KF)[0:64,
                                                  pc * T + qc * TCW:
                                                  pc * T + (qc + 1) * TCW]
                    om = P.put(v, _v1("tensor_mul", dst, obank[0:64, 0:512],
                                      SGBF[:]), [cp], sig=True)
                    bset(obank, om)
                    state["last_om"] = om

        xr_last = state["xr"]
        wo_cons = [last_sg, last_sg]
        for q4 in range(4):
            sl = q4 % 2
            woslot = AR3[:, sl * 4096:(sl + 1) * 4096]
            wts = []
            for h in range(16):
                wts.append(DMA(g, woslot[0:64, h * 256:(h + 1) * 256],
                               wo_in[h * 64:(h + 1) * 64,
                                     q4 * 256:(q4 + 1) * 256],
                               waits=[wo_cons[sl], last_sg]))
            wtok = wts[-1]
            for fc2 in range(2):
                fcg = q4 * 2 + fc2
                for tc in range(NTC):
                    bank = PC if (fcg * NTC + tc) % 2 == 0 else PD
                    mmt = None
                    for h in range(16):
                        l = woslot[0:64, h * 256 + fc2 * 128:
                                   h * 256 + (fc2 + 1) * 128]
                        r = (QO if h % 2 == 0 else KF)[0:64,
                                                       (h // 2) * T + tc * TCW:
                                                       (h // 2) * T + (tc + 1) * TCW]
                        mmt = P.put(t, _mm(bank[:, 0:TCW], l, r,
                                           h == 0, h == 15),
                                    [wtok, bfree(bank), state["last_om"]],
                                    sig=(h == 15))
                    wo_cons[sl] = mmt
                    xs = XR[:, fcg * T + tc * TCW: fcg * T + (tc + 1) * TCW]
                    xr_last = P.put(v, _v1("tensor_add", xs, xs, bank[:, 0:TCW]),
                                    [mmt, xr_last], sig=True)
                    bset(bank, xr_last)
        state["xr"] = xr_last
        state["ffn_free"] = xr_last

    final_toks = []
    with ctx:
        with nc.allow_low_precision(reason="bf16 matmuls with fp32 accumulation"):
            with nc.Block() as block:
                for b in range(2):
                    io_in(b)
                    if stage >= 1:
                        ssm(b)
                    if stage >= 2:
                        ffn(b, 0)
                    if stage >= 3:
                        attn(b)
                    if stage >= 4:
                        ffn(b, 1)
                    final_toks += io_out(b)
                P.put(y, lambda e: e.nop(), final_toks)
                P.emit(block)
    return nc


def kernel(**inputs):
    x = np.ascontiguousarray(inputs["L0_attn_a_x"])
    consts = {
        "c_idf": np.eye(128, dtype=np.float32),
        "c_idb": np.eye(128, dtype=BFNP),
        "c_masku": np.triu(np.ones((128, 128), dtype=np.float32)).astype(BFNP),
        "c_sel8": np.kron(np.eye(8, dtype=np.float32),
                          np.ones((1, 128), dtype=np.float32)).astype(BFNP),
    }
    base = {}
    for k, a in inputs.items():
        if k == "L0_attn_a_x":
            continue
        a = np.ascontiguousarray(a)
        if k == "L0_attn_b_active_mask":
            a = a.view(np.uint8)
        base[k] = a
    base.update(consts)
    nc = build()
    in_maps = []
    for c in range(8):
        m = dict(base)
        m["L0_attn_a_x"] = np.ascontiguousarray(x[2 * c:2 * c + 2])
        in_maps.append(m)
    res = run_bass_kernel_spmd(nc, in_maps, core_ids=list(range(8)))
    return np.concatenate([r["out"] for r in res.results], axis=0)
